# version 07 (final)
# baseline (speedup 1.0000x reference)
import numpy as np
import ml_dtypes
import concourse.bass as bass
import concourse.mybir as mybir
from concourse.bass_utils import run_bass_kernel_spmd

F32 = mybir.dt.float32
BF16 = mybir.dt.bfloat16
FP8 = mybir.dt.float8e5
AF = mybir.ActivationFunctionType
ALU = mybir.AluOpType

B, T, D = 4, 4096, 1024
H, DH = 16, 64
FF = 4096
TOK = 2048
NCH = 8
NPAIR = 8
KC = 32
CLAMP = 2046

KBL = [max(0, -(-(256 * j - 1794) // 128)) for j in range(16)]
WWIN = [kbl * 128 + 128 if kbl > 0 else 0 for kbl in KBL]
WOFF = np.concatenate([[0], np.cumsum(WWIN)]).astype(int)
SUMW = int(WOFF[-1])
BAND_J = [j for j in range(16) if KBL[j] > 0]
SOFF = {}
_acc = 0
for _j in BAND_J:
    SOFF[_j] = _acc
    _acc += KBL[_j] * 128
STAGE_W = _acc
PERM_GC = [2 * s for s in range(16)] + [2 * s + 1 for s in range(16)]

DEPTH_LIMIT = 2
DEBUG_TAPS = 0
PHASE_LIMIT = 99
ATTN_REL = True
ATTN_PAIRS = 8
ATTN_DELTA = True
ATTN_SKEW = True
ATTN_PP = True


def _threefry2x32(k1, k2, x0, x1):
    x0 = np.asarray(x0, np.uint32).copy()
    x1 = np.asarray(x1, np.uint32).copy()
    ks0 = np.uint32(k1)
    ks1 = np.uint32(k2)
    ks2 = np.uint32(0x1BD11BDA) ^ ks0 ^ ks1
    x0 += ks0
    x1 += ks1
    R = [[13, 15, 26, 6], [17, 29, 16, 24]]
    inj = [(ks1, ks2, 1), (ks2, ks0, 2), (ks0, ks1, 3), (ks1, ks2, 4),
           (ks2, ks0, 5)]
    for i in range(5):
        for r in R[i % 2]:
            x0 += x1
            x1 = (x1 << np.uint32(r)) | (x1 >> np.uint32(32 - r))
            x1 ^= x0
        a, b, c = inj[i]
        x0 += a
        x1 += b + np.uint32(c)
    return x0, x1


def _dropout_keepmask():
    n = B * T * D
    idx = np.arange(n, dtype=np.uint64)
    c1 = (idx >> np.uint64(32)).astype(np.uint32)
    c2 = (idx & np.uint64(0xFFFFFFFF)).astype(np.uint32)
    b1, b2 = _threefry2x32(0, 42, c1, c2)
    bits = b1 ^ b2
    u = ((bits >> np.uint32(9)) | np.uint32(0x3F800000)).view(np.float32) \
        - np.float32(1.0)
    return (u < np.float32(0.8)).reshape(B, T, D)


def _rotary_tables():
    inv = (1.0 / 10000.0 ** (np.arange(0, DH, 2, dtype=np.float32) / DH)
           ).astype(np.float32)
    t = np.arange(T, dtype=np.float32)
    freqs = np.einsum("i,j->ij", t, inv).astype(np.float32)
    emb = np.concatenate([freqs, freqs], -1)
    return np.cos(emb).astype(np.float32), np.sin(emb).astype(np.float32)


def _chunkvec(v):
    return np.ascontiguousarray(np.asarray(v, np.float32).reshape(-1, 128).T)


def build_nc():
    nc = bass.Bass("TRN2", target_bir_lowering=False, debug=False,
                   detect_race_conditions=False)

    def dinp(name, shape, dtype=F32):
        return nc.dram_tensor(name, shape, dtype, kind="ExternalInput").ap()

    x_in = dinp("x_in", [NCH, 128, TOK])
    wq = dinp("wq", [D, D])
    wk = dinp("wk", [D, D])
    wv = dinp("wv", [D, D])
    wo = dinp("wo", [D, D])
    ew = dinp("ew", [128, SUMW])
    e2046 = dinp("e2046", [128, 1])
    cos_t = dinp("cos_t", [128, TOK])
    sin_t = dinp("sin_t", [128, TOK])
    w1t = dinp("w1t", [D, FF])
    b1_t = dinp("b1_t", [128, FF // 128])
    w2t = dinp("w2t", [FF, D])
    dropm = dinp("dropm", [NCH, 128, TOK], BF16)
    wqkv = dinp("wqkv", [D, 3 * D])
    bv_b = dinp("bv_b", [128, D])
    wproj = dinp("wproj", [D, D])
    wft = dinp("wft", [D, D])
    ident_in = dinp("ident_in", [128, 128])
    prot_in = dinp("prot_in", [128, 128])
    ones_in = dinp("ones_in", [128, 1])
    ones1x = dinp("ones1x", [1, 128])
    ones64x = dinp("ones64x", [1, 64])
    biases_in = dinp("biases_in", [128, 64])

    out_t = nc.dram_tensor("out", [NCH, 128, TOK], F32,
                           kind="ExternalOutput").ap()
    taps = {}
    for i in range(1, DEBUG_TAPS + 1):
        taps[i] = nc.dram_tensor("tap%d" % i, [NCH, 128, TOK], F32,
                                 kind="ExternalOutput").ap()

    q_dram = nc.dram_tensor("q_dram", [NPAIR, 128, TOK], F32,
                            kind="Internal").ap()
    KSEC = NPAIR * 128 * TOK
    VSEC = (TOK // 128) * 128 * D
    cc_in = nc.dram_tensor("cc_in", [KSEC + VSEC], F32, kind="Internal").ap()
    cc_out = nc.dram_tensor("cc_out", [2, KSEC + VSEC], F32,
                            kind="Internal").ap()
    pp_dram = nc.dram_tensor("pp_dram", [2, 128, SUMW], F32,
                             kind="Internal").ap()

    XR = nc.alloc_sbuf_tensor("xres", [128, NCH * TOK], F32).ap()
    Aa = nc.alloc_sbuf_tensor("arA", [128, NCH * TOK], F32).ap()
    Bu = nc.alloc_sbuf_tensor("arB", [128, NCH * TOK], F32).ap()
    Oo = nc.alloc_sbuf_tensor("arO", [128, 2048], F32).ap()
    ident = nc.alloc_sbuf_tensor("ident", [128, 128], F32).ap()
    prot = nc.alloc_sbuf_tensor("prot", [128, 128], F32).ap()
    onesc = nc.alloc_sbuf_tensor("onesc", [128, 1], F32).ap()
    ones1 = nc.alloc_sbuf_tensor("ones1", [1, 128], F32).ap()
    ones64 = nc.alloc_sbuf_tensor("ones64", [128, 64], F32).ap()
    biases = nc.alloc_sbuf_tensor("biases", [128, 64], F32).ap()
    b1sb = nc.alloc_sbuf_tensor("b1sb", [128, FF // 128], F32).ap()
    bvb = nc.alloc_sbuf_tensor("bvb", [128, D], F32).ap()
    e2sb = nc.alloc_sbuf_tensor("e2sb", [128, 1], F32).ap()

    ps = [nc.alloc_psum_tensor("ps%d" % i, [128, 1024], F32).ap()
          for i in range(4)]

    BO = biases[:, 0:8]
    BQ = biases[:, 8:16]
    BK = biases[:, 16:24]
    BPROJ = biases[:, 32:40]
    LNW = biases[:, 40:48]
    LNB = biases[:, 48:56]
    B2C = biases[:, 56:64]
    EPS = biases[:, 24:25]

    sems = {}

    class S:
        __slots__ = ("h", "v")

        def __init__(self, name):
            self.h = nc.alloc_semaphore(name)
            self.v = 0

    def sem(name):
        if name not in sems:
            sems[name] = S(name)
        return sems[name]

    def inc(s, instr, n=1):
        instr.then_inc(s.h, n)
        s.v += n
        return s.v

    def dinc(s, instr):
        return inc(s, instr, 16)

    with nc.Block(no_gpsimd_drain=True) as blk:
        @blk.sync
        def _(sy):
            ld = sem("pre")
            for dst, src in [
                (ident, ident_in), (prot, prot_in), (onesc, ones_in),
                (ones1, ones1x), (biases, biases_in), (b1sb, b1_t),
                (bvb, bv_b), (e2sb, e2046),
            ]:
                dinc(ld, sy.dma_start(out=dst[:], in_=src[:]))
            dinc(ld, sy.dma_start(out=ones64[64:65, :], in_=ones64x[:]))
            for c in range(NCH):
                dinc(ld, sy.dma_start(out=XR[:, c * TOK:(c + 1) * TOK],
                                      in_=x_in[c]))
            sy.wait_ge(ld.h, ld.v)

    ln_n = [0]

    def emit_ln(apply_cb):
        i = ln_n[0]
        ln_n[0] += 1
        lq = sem("lnq")
        lm = sem("lnm")
        lr = sem("lnr")
        for half in (0, 1):
            hb = half * 1024
            q0, m0, r0 = lq.v, lm.v, lr.v
            with nc.Block(no_gpsimd_drain=True) as blk:
                @blk.scalar
                def _(sc, hb=hb, q0=q0, m0=m0, r0=r0):
                    for c in range(NCH):
                        if c >= 2:
                            sc.wait_ge(lm.h, m0 + c - 1)
                        inc(lq, sc.activation(
                            Bu[:, (c % 2) * 1024:(c % 2) * 1024 + 1024],
                            XR[:, c * TOK + hb:c * TOK + hb + 1024],
                            AF.Square))
                    sc.wait_ge(lr.h, r0 + 1)
                    inc(lq, sc.activation(
                        Bu[0:1, 4096:5120], Bu[0:1, 4096:5120],
                        AF.Sqrt, bias=EPS[0:1, :]))

                @blk.tensor
                def _(te, hb=hb, q0=q0, m0=m0, r0=r0):
                    for c in range(NCH):
                        for n2 in range(2):
                            te.matmul(ps[2][0:1, n2 * 512:n2 * 512 + 512],
                                      onesc,
                                      XR[:, c * TOK + hb + n2 * 512:
                                         c * TOK + hb + n2 * 512 + 512],
                                      start=(c == 0), stop=(c == NCH - 1))
                        te.wait_ge(lq.h, q0 + c + 1)
                        for n2 in range(2):
                            w = te.matmul(
                                ps[3][0:1, n2 * 512:n2 * 512 + 512],
                                onesc,
                                Bu[:, (c % 2) * 1024 + n2 * 512:
                                   (c % 2) * 1024 + n2 * 512 + 512],
                                start=(c == 0), stop=(c == NCH - 1))
                        inc(lm, w)
                    te.wait_ge(lr.h, r0 + 2)
                    for n2 in range(2):
                        te.matmul(ps[0][:, n2 * 512:n2 * 512 + 512], ones1,
                                  Bu[0:1, 3072 + n2 * 512:3072 + n2 * 512 + 512],
                                  start=True, stop=True)
                        w = te.matmul(ps[1][:, n2 * 512:n2 * 512 + 512], ones1,
                                      Bu[0:1, 4096 + n2 * 512:
                                         4096 + n2 * 512 + 512],
                                      start=True, stop=True)
                    inc(lm, w)

                @blk.vector
                def _(ve, c_hb=hb, q0=q0, m0=m0, r0=r0):
                    hb = c_hb
                    ve.wait_ge(lm.h, m0 + NCH)
                    ve.tensor_scalar_mul(Bu[0:1, 3072:4096],
                                         ps[2][0:1, 0:1024], 1.0 / 1024.0)
                    ve.tensor_scalar_mul(Bu[0:1, 4096:5120],
                                         ps[3][0:1, 0:1024], 1.0 / 1024.0)
                    ve.tensor_mul(Bu[0:1, 5120:6144], Bu[0:1, 3072:4096],
                                  Bu[0:1, 3072:4096])
                    w = ve.tensor_sub(Bu[0:1, 4096:5120], Bu[0:1, 4096:5120],
                                      Bu[0:1, 5120:6144])
                    inc(lr, w)
                    ve.wait_ge(lq.h, q0 + NCH + 1)
                    w = ve.reciprocal(Bu[0:1, 4096:5120], Bu[0:1, 4096:5120])
                    inc(lr, w)
                    ve.wait_ge(lm.h, m0 + NCH + 1)
                    for c in range(NCH):
                        ve.tensor_sub(Bu[:, 2048:3072],
                                      XR[:, c * TOK + hb:c * TOK + hb + 1024],
                                      ps[0][:, 0:1024])
                        apply_cb(ve, c, half, Bu[:, 2048:3072],
                                 ps[1][:, 0:1024])

    def emit_qkv(layer):
        sw = [sem("qw0"), sem("qw1")]
        scs = sem("qcs")
        smm = sem("qmm")
        smr = sem("qmr")
        sraw = sem("qraw")
        scb = sem("qcb")
        sst = sem("qst")

        if layer == 0:
            jobs = [(wq, 0, True, "q", 0, None), (wq, 512, True, "q", 4, None),
                    (wk, 0, True, "k", 0, None), (wk, 512, True, "k", 4, None)]
            vmat, vcol0, vbias = wv, 0, False
        else:
            jobs = [(wqkv, 0, False, "q", 0, BQ), (wqkv, 512, False, "q", 4, BQ),
                    (wqkv, D, False, "k", 0, BK),
                    (wqkv, D + 512, False, "k", 4, BK)]
            vmat, vcol0, vbias = wqkv, 2 * D, True

        def wsrc(mat_ap, col0):
            t = mat_ap.tensor
            ncol = t.shape[1]
            return bass.AP(tensor=t, offset=col0,
                           ap=[[ncol, 128], [128 * ncol, 8], [1, 512]])

        def dst_ap(dst, oc_glob, tt):
            tn = q_dram.tensor if dst == "q" else cc_in.tensor
            return bass.AP(tensor=tn, offset=oc_glob * 128 * TOK + tt * 512,
                           ap=[[TOK, 128], [1, 512]])

        mm0, mr0, cb0, st0 = smm.v, smr.v, scb.v, sst.v
        raw0 = sraw.v
        w0b = [sw[0].v, sw[1].v]
        rix = {}
        _r = 0
        for ji, job in enumerate(jobs):
            if job[2]:
                for s in range(16):
                    rix[(ji, s)] = _r
                    _r += 1
        RAW0, QF0, VS0 = 12288, 14336, 15360

        with nc.Block(no_gpsimd_drain=True) as blk:
            @blk.sync
            def _(sy):
                if layer == 0:
                    dinc(scs, sy.dma_start(out=Bu[:, 8192:8192 + TOK],
                                           in_=cos_t[:]))
                    dinc(scs, sy.dma_start(out=Bu[:, 10240:10240 + TOK],
                                           in_=sin_t[:]))
                for ji, job in enumerate(jobs):
                    buf = ji % 2
                    if ji >= 2:
                        sy.wait_ge(smm.h, mm0 + (ji - 1) * 16)
                    dinc(sw[buf], sy.dma_start(
                        out=Bu[:, buf * 4096:(buf + 1) * 4096],
                        in_=wsrc(job[0], job[1])))
                for vh in range(2):
                    jv = len(jobs) + vh
                    buf = jv % 2
                    sy.wait_ge(smm.h, mm0 + (jv - 1) * 16)
                    dinc(sw[buf], sy.dma_start(
                        out=Bu[:, buf * 4096:(buf + 1) * 4096],
                        in_=wsrc(vmat, vcol0 + vh * 512)))
            @blk.gpsimd
            def _(gp):
                for ji, job in enumerate(jobs):
                    for s in range(16):
                        g = ji * 16 + s
                        oc, tt = s // 4, s % 4
                        gp.wait_ge(scb.h, cb0 + g + 1)
                        dinc(sst, gp.dma_start(
                            out=dst_ap(job[3], job[4] + oc, tt),
                            in_=Bu[:, QF0 + (g % 2) * 512:
                                   QF0 + (g % 2) * 512 + 512]))
                for vh in range(2):
                    for tc in range(16):
                        gv = vh * 16 + tc
                        gp.wait_ge(scb.h, cb0 + 64 + gv + 1)
                        dinc(sst, gp.dma_start(
                            out=bass.AP(tensor=cc_in.tensor,
                                        offset=KSEC + tc * 128 * D + vh * 512,
                                        ap=[[D, 128], [1, 512]]),
                            in_=Bu[:, VS0 + (gv % 2) * 512:
                                   VS0 + (gv % 2) * 512 + 512]))

            @blk.tensor
            def _(te):
                if layer == 0:
                    te.wait_ge(scs.h, scs.v)

                def rot_mm(ji, gp):
                    sp = gp - ji * 16
                    te.wait_ge(sraw.h, raw0 + rix[(ji, sp)] + 1)
                    if gp >= 2:
                        te.wait_ge(scb.h, cb0 + gp - 2 + 1)
                    w = te.matmul(
                        ps[2][:, (gp % 2) * 512:(gp % 2) * 512 + 512],
                        prot,
                        Bu[:, RAW0 + (gp % 4) * 512:RAW0 + (gp % 4) * 512 + 512],
                        start=True, stop=True)
                    inc(smr, w)

                for ji, job in enumerate(jobs):
                    mat, col0, rot, dst, oc0, bias = job
                    buf = ji % 2
                    te.wait_ge(sw[buf].h, w0b[buf] + 16 * (ji // 2 + 1))
                    for s in range(16):
                        g = ji * 16 + s
                        oc, tt = s // 4, s % 4
                        r = g % 4
                        pp = ps[r // 2][:, (r % 2) * 512:(r % 2) * 512 + 512]
                        if g >= 4:
                            te.wait_ge(scb.h, cb0 + g - 4 + 1)
                        for dc in range(NCH):
                            w = te.matmul(
                                pp,
                                Bu[:, buf * 4096 + dc * 512 + oc * 128:
                                   buf * 4096 + dc * 512 + oc * 128 + 128],
                                Aa[:, dc * TOK + tt * 512:
                                   dc * TOK + tt * 512 + 512],
                                start=(dc == 0), stop=(dc == NCH - 1))
                        inc(smm, w)
                        if rot and s >= 1:
                            rot_mm(ji, g - 1)
                    if rot:
                        rot_mm(ji, ji * 16 + 15)
                te.wait_ge(scb.h, cb0 + 64)
                for vh in range(2):
                    jv = len(jobs) + vh
                    buf = jv % 2
                    te.wait_ge(sw[buf].h, w0b[buf] + 16 * (jv // 2 + 1))
                    for tc in range(16):
                        gv = vh * 16 + tc
                        r = gv % 4
                        pp = ps[2 + r // 2][:, (r % 2) * 512:(r % 2) * 512 + 512]
                        if gv >= 4:
                            te.wait_ge(scb.h, cb0 + 64 + gv - 4 + 1)
                        for dc in range(NCH):
                            w = te.matmul(
                                pp,
                                Aa[:, dc * TOK + tc * 128:
                                   dc * TOK + tc * 128 + 128],
                                Bu[:, buf * 4096 + dc * 512:
                                   buf * 4096 + dc * 512 + 512],
                                start=(dc == 0), stop=(dc == NCH - 1))
                        inc(smm, w)

            @blk.vector
            def _(ve):
                def combine(ji, gp):
                    sp = gp - ji * 16
                    ttp = sp % 4
                    ve.wait_ge(smr.h, mr0 + rix[(ji, sp)] + 1)
                    if gp >= 2:
                        ve.wait_ge(sst.h, st0 + 16 * (gp - 2 + 1))
                    ve.tensor_mul(
                        Bu[:, QF0 + (gp % 2) * 512:QF0 + (gp % 2) * 512 + 512],
                        Bu[:, RAW0 + (gp % 4) * 512:RAW0 + (gp % 4) * 512 + 512],
                        Bu[:, 8192 + ttp * 512:8192 + ttp * 512 + 512])
                    ve.tensor_mul(
                        Bu[:, VS0 + 512:VS0 + 1024],
                        ps[2][:, (gp % 2) * 512:(gp % 2) * 512 + 512],
                        Bu[:, 10240 + ttp * 512:10240 + ttp * 512 + 512])
                    w = ve.tensor_add(
                        Bu[:, QF0 + (gp % 2) * 512:QF0 + (gp % 2) * 512 + 512],
                        Bu[:, QF0 + (gp % 2) * 512:QF0 + (gp % 2) * 512 + 512],
                        Bu[:, VS0 + 512:VS0 + 1024])
                    inc(scb, w)

                for ji, job in enumerate(jobs):
                    mat, col0, rot, dst, oc0, bias = job
                    for s in range(16):
                        g = ji * 16 + s
                        oc, tt = s // 4, s % 4
                        r = g % 4
                        pp = ps[r // 2][:, (r % 2) * 512:(r % 2) * 512 + 512]
                        ve.wait_ge(smm.h, mm0 + g + 1)
                        if rot:
                            if g >= 4:
                                ve.wait_ge(smr.h, mr0 + rix[(ji, s)] - 4 + 1)
                            w = ve.tensor_copy(
                                Bu[:, RAW0 + (g % 4) * 512:
                                   RAW0 + (g % 4) * 512 + 512], pp)
                            inc(sraw, w)
                            if s >= 1:
                                combine(ji, g - 1)
                            if s == 15:
                                combine(ji, g)
                        else:
                            if g >= 2:
                                ve.wait_ge(sst.h, st0 + 16 * (g - 2 + 1))
                            w = ve.tensor_scalar(
                                Bu[:, QF0 + (g % 2) * 512:
                                   QF0 + (g % 2) * 512 + 512],
                                pp, bias[:, oc0 + oc:oc0 + oc + 1],
                                None, ALU.add)
                            inc(scb, w)
                for vh in range(2):
                    for tc in range(16):
                        gv = vh * 16 + tc
                        r = gv % 4
                        pp = ps[2 + r // 2][:, (r % 2) * 512:(r % 2) * 512 + 512]
                        ve.wait_ge(smm.h, mm0 + 64 + gv + 1)
                        if gv >= 2:
                            ve.wait_ge(sst.h, st0 + 16 * 64 + 16 * (gv - 2 + 1))
                        if vbias:
                            w = ve.tensor_add(
                                Bu[:, VS0 + (gv % 2) * 512:
                                   VS0 + (gv % 2) * 512 + 512],
                                pp, bvb[:, vh * 512:vh * 512 + 512])
                        else:
                            w = ve.tensor_copy(
                                Bu[:, VS0 + (gv % 2) * 512:
                                   VS0 + (gv % 2) * 512 + 512], pp)
                        inc(scb, w)

    def emit_allgather():
        scc = sem("cc")
        sst = sem("qst")
        with nc.Block(no_gpsimd_drain=True) as blk:
            @blk.gpsimd
            def _(gp):
                gp.wait_ge(sst.h, sst.v)
                w = gp.collective_compute(
                    "AllGather", ALU.bypass,
                    replica_groups=[[0, 1], [2, 3], [4, 5], [6, 7]],
                    ins=[cc_in[:]], outs=[cc_out[:]])
                inc(scc, w)
                gp.wait_ge(scc.h, scc.v)

    E8_0, PP_0, STG_0 = 0, 2304, 4608
    KP_0, VA_0, VB_0, QP_0, AR_0, OT_0 = 0, 4096, 6176, 8256, 10304, 13376

    def emit_attention(layer):
        wmat = wo if layer == 0 else wproj
        obias = BO if layer == 0 else BPROJ
        rel = (layer == 0) and ATTN_REL
        NB = len(BAND_J)
        NW = {j: -(-WWIN[j] // 512) for j in BAND_J}
        NPW = sum(NW.values())

        sld = sem("ald")
        se8 = sem("ae8")
        spp = sem("app")
        spe = sem("apev")
        spd = sem("apst")
        skw = sem("askw")
        smq = sem("amq")
        sxp = sem("axp")
        spv = sem("apv")
        ssc = sem("asc")
        sbc = sem("abc")
        sdv = sem("adv")
        sop = sem("aop")
        sxa = sem("axa")

        def kslice(p, half):
            return bass.AP(tensor=cc_out.tensor,
                           offset=half * (KSEC + VSEC) + p * 128 * TOK,
                           ap=[[TOK, 128], [1, TOK]])

        def vsrc(half, hglob):
            return bass.AP(tensor=cc_out.tensor,
                           offset=half * (KSEC + VSEC) + KSEC + hglob * 64,
                           ap=[[D, 128], [128 * D, 16], [1, 64]])

        def vdst(vbase, half):
            return bass.AP(tensor=Bu.tensor, offset=vbase + half * 16 * 65,
                           ap=[[NCH * TOK, 128], [65, 16], [1, 64]])

        def ppc(ji, h, wch):
            j = BAND_J[ji]
            return sum(2 * NW[BAND_J[k]] for k in range(ji)) \
                + h * NW[j] + wch + 1

        for p in range(min(NPAIR, ATTN_PAIRS)):
            ld0, e80 = sld.v, se8.v
            pp0, pe0, pd0, kw0 = spp.v, spe.v, spd.v, skw.v
            with nc.Block(no_gpsimd_drain=True) as blk:
                @blk.sync
                def _(sy, p=p, ld0=ld0, pp0=pp0, pe0=pe0):
                    sy.wait_ge(sem("qst").h, sems["qst"].v)
                    dinc(sld, sy.dma_start(out=Bu[:, QP_0:QP_0 + TOK],
                                           in_=q_dram[p]))
                    for h in range(2):
                        for half in range(2):
                            dinc(sld, sy.dma_start(
                                out=vdst(VA_0 if h == 0 else VB_0, half),
                                in_=vsrc(half, 2 * p + h)))
                    for h in range(2):
                        dinc(sld, sy.dma_start(
                            out=Oo[0:64, h * 1024:h * 1024 + 1024],
                            in_=bass.AP(tensor=wmat.tensor,
                                        offset=(p * 128 + h * 64) * D,
                                        ap=[[D, 64], [1, 1024]])))
                    if rel:
                        for ji, j in enumerate(BAND_J):
                            if ji >= 1:
                                sy.wait_ge(spp.h, pp0 + ppc(ji - 1, 1,
                                                            NW[BAND_J[ji - 1]]
                                                            - 1))
                            dinc(sld, sy.dma_start(
                                out=Aa[:, E8_0:E8_0 + WWIN[j]],
                                in_=ew[:, int(WOFF[j]):
                                       int(WOFF[j]) + WWIN[j]]))

                @blk.vector
                def _(ve, p=p, ld0=ld0, pp0=pp0, pd0=pd0):
                    ve.wait_ge(sld.h, ld0 + 16 * 5)
                    for vbase in (VA_0, VB_0):
                        ones_ap = bass.AP(tensor=Bu.tensor,
                                          offset=vbase + 64,
                                          ap=[[NCH * TOK, 128], [65, 32],
                                              [1, 1]])
                        ve.memset(ones_ap, 1.0)
                    if rel:
                        for ji, j in enumerate(BAND_J):
                            ve.wait_ge(sld.h, ld0 + 16 * (7 + ji + 1))
                            w = ve.tensor_scalar(
                                Aa[:, E8_0:E8_0 + WWIN[j]],
                                Aa[:, E8_0:E8_0 + WWIN[j]],
                                e2sb[:, 0:1], 8.0, ALU.subtract, ALU.mult)
                            inc(se8, w)
                            for h in range(2):
                                seqi = 2 * ji + h
                                if seqi >= 1:
                                    ve.wait_ge(spd.h, pd0 + 16 * seqi)
                                for wch in range(NW[j]):
                                    ve.wait_ge(spp.h, pp0 + ppc(ji, h, wch))
                                    wid = min(512, WWIN[j] - wch * 512)
                                    w = ve.tensor_copy(
                                        Aa[:, PP_0 + wch * 512:
                                           PP_0 + wch * 512 + wid],
                                        ps[3][:, (wch % 2) * 512:
                                              (wch % 2) * 512 + wid])
                                    inc(spe, w)


                if rel:
                    @blk.tensor
                    def _(te, p=p, e80=e80, pe0=pe0, ld0=ld0):
                        te.wait_ge(sld.h, ld0 + 16)
                        last_reg = {0: None, 1: None}
                        for ji, j in enumerate(BAND_J):
                            te.wait_ge(se8.h, e80 + ji + 1)
                            for h in range(2):
                                for wch in range(NW[j]):
                                    gidx = ppc(ji, h, wch)
                                    r = wch % 2
                                    if last_reg[r] is not None:
                                        te.wait_ge(spe.h, pe0 + last_reg[r])
                                    last_reg[r] = gidx
                                    wid = min(512, WWIN[j] - wch * 512)
                                    w = te.matmul(
                                        ps[3][:, r * 512:r * 512 + wid],
                                        Bu[h * 64:h * 64 + 64,
                                           QP_0 + j * 128:QP_0 + j * 128 + 128],
                                        Aa[h * 64:h * 64 + 64,
                                           E8_0 + wch * 512:
                                           E8_0 + wch * 512 + wid],
                                        start=True, stop=True)
                                    inc(spp, w)

                    @blk.gpsimd
                    def _(gp, p=p, pd0=pd0, pe0=pe0):
                        gp.wait_ge(skw.h, skw.v)
                        for ji, j in enumerate(BAND_J):
                            for h in range(2):
                                gp.wait_ge(spe.h, pe0 + ppc(ji, h, NW[j] - 1))
                                dinc(spd, gp.dma_start(
                                    out=bass.AP(
                                        tensor=pp_dram.tensor,
                                        offset=h * 128 * SUMW + int(WOFF[j]),
                                        ap=[[SUMW, 128], [1, WWIN[j]]]),
                                    in_=Aa[:, PP_0:PP_0 + WWIN[j]]))
                        for ji, j in enumerate(BAND_J):
                            gp.wait_ge(spd.h, pd0 + 16 * (2 * ji + 1))
                            if not ATTN_SKEW:
                                continue
                            dinc(skw, gp.dma_start(
                                out=Aa[:, STG_0 + SOFF[j]:
                                       STG_0 + SOFF[j] + KBL[j] * 128],
                                in_=bass.AP(
                                    tensor=pp_dram.tensor,
                                    offset=int(WOFF[j]) + 127,
                                    ap=[[SUMW - 1, 128], [1, KBL[j] * 128]])))

            mq0, xp0, pv0 = smq.v, sxp.v, spv.v
            sc0, bc0, dv0 = ssc.v, sbc.v, sdv.v
            op0, xa0 = sop.v, sxa.v
            kwA = kw0 + 16 * NB
            kwB = kw0 + 32 * NB
            smq_d = {}
            sbc_d = {}
            sop_d = {}
            ldm0 = sld.v
            with nc.Block(no_gpsimd_drain=True) as blk:
                @blk.sync
                def _(sy, p=p, mq0=mq0):
                    for h in range(2):
                        if h == 1:
                            sy.wait_ge(smq.h, mq0 + 64)
                        for half in range(2):
                            for dup in range(2):
                                dinc(sld, sy.dma_start(
                                    out=bass.AP(
                                        tensor=Bu.tensor,
                                        offset=KP_0 + half * TOK
                                        + dup * 64 * NCH * TOK,
                                        ap=[[NCH * TOK, 64], [1, TOK]]),
                                    in_=bass.AP(
                                        tensor=cc_out.tensor,
                                        offset=half * (KSEC + VSEC)
                                        + (p * 128 + h * 64) * TOK,
                                        ap=[[TOK, 64], [1, TOK]])))
                        for dup in range(2):
                            dinc(sld, sy.dma_start(
                                out=bass.AP(
                                    tensor=Bu.tensor,
                                    offset=QP_0 + dup * 64 * NCH * TOK,
                                    ap=[[NCH * TOK, 64], [1, TOK]]),
                                in_=bass.AP(
                                    tensor=q_dram.tensor,
                                    offset=(p * 128 + h * 64) * TOK,
                                    ap=[[TOK, 64], [1, TOK]])))

                @blk.tensor
                def _(te, p=p, ld0=ld0, ldm0=ldm0, mq0=mq0, xp0=xp0, pv0=pv0,
                      sc0=sc0, bc0=bc0, dv0=dv0, op0=op0, xa0=xa0,
                      kwA=kwA, kwB=kwB):
                    for h in range(2):
                        te.wait_ge(sld.h, ldm0 + 16 * 6 * (h + 1))
                        vbase = VA_0 if h == 0 else VB_0
                        if rel and ATTN_SKEW:
                            te.wait_ge(skw.h, kwA if h == 0 else kwB)
                        for u in range(4):
                            hu = h * 4 + u
                            for g in range(16):
                                G = hu * 16 + g
                                if G >= 2:
                                    te.wait_ge(sxp.h, xp0 + G - 1)
                                for j2 in range(2):
                                    kc = 2 * g + j2
                                    gc = PERM_GC[kc]
                                    dl = []
                                    if rel and ATTN_DELTA:
                                        for jj in range(4):
                                            j = 4 * u + jj
                                            if j in SOFF and gc < KBL[j]:
                                                dl.append((j, jj))
                                    w = te.matmul(
                                        ps[G % 2][:, j2 * 512:j2 * 512 + 512],
                                        Bu[j2 * 64:j2 * 64 + 64,
                                           KP_0 + kc * 128:KP_0 + kc * 128 + 128],
                                        Bu[j2 * 64:j2 * 64 + 64,
                                           QP_0 + u * 512:QP_0 + u * 512 + 512],
                                        start=True, stop=(len(dl) == 0))
                                    for di, (j, jj) in enumerate(dl):
                                        w = te.matmul(
                                            ps[G % 2][:, j2 * 512 + jj * 128:
                                                  j2 * 512 + jj * 128 + 128],
                                            Aa[:, STG_0 + SOFF[j] + gc * 128:
                                               STG_0 + SOFF[j] + gc * 128 + 128],
                                            ident,
                                            start=False,
                                            stop=(di == len(dl) - 1))
                                smq_d[(p, h, u, g)] = inc(smq, w)
                                te.wait_ge(sxp.h, xp0 + G + 1)
                                if g == 0:
                                    if hu >= 2:
                                        te.wait_ge(sdv.h, dv0 + hu - 1)
                                for j2 in range(2):
                                    kc = 2 * g + j2
                                    w = te.matmul(
                                        ps[2][0:65, (u % 2) * 512:
                                              (u % 2) * 512 + 512],
                                        Bu[:, vbase + kc * 65:
                                           vbase + kc * 65 + 65],
                                        Bu[:, AR_0 + (G % 3) * 1024 + j2 * 512:
                                           AR_0 + (G % 3) * 1024 + j2 * 512
                                           + 512],
                                        start=(g == 0 and j2 == 0),
                                        stop=(g == 15 and j2 == 1))
                                inc(spv, w)
                            te.wait_ge(ssc.h, sc0 + hu + 1)
                            if hu >= 2:
                                te.wait_ge(sdv.h, dv0 + hu - 1)
                            if h == 1 and u == 0:
                                te.wait_ge(sxa.h, xa0 + 32)
                            w = te.matmul(
                                ps[3][0:64, (u % 2) * 512:(u % 2) * 512 + 512],
                                ones64[64:65, 0:64],
                                Oo[64:65, u * 512:u * 512 + 512],
                                start=True, stop=True)
                            sbc_d[(p, h, u)] = inc(sbc, w)
                        te.wait_ge(sdv.h, dv0 + h * 4 + 4)
                        for dc in range(NCH):
                            for u in range(4):
                                i = dc * 4 + u
                                if i >= 2:
                                    te.wait_ge(sxa.h, xa0 + h * 32 + i - 1)
                                w = te.matmul(
                                    ps[3][:, (i % 2) * 512:(i % 2) * 512 + 512],
                                    Oo[0:64, h * 1024 + dc * 128:
                                       h * 1024 + dc * 128 + 128],
                                    Bu[0:64, OT_0 + u * 512:
                                       OT_0 + u * 512 + 512],
                                    start=True, stop=True)
                                sop_d[(p, h, dc, u)] = inc(sop, w)

                @blk.gpsimd
                def _(gp, p=p, pd0=pd0):
                    if rel and ATTN_SKEW:
                        gp.wait_ge(smq.h, smq_d[(p, 0, 3, 15)])
                        for ji, j in enumerate(BAND_J):
                            dinc(skw, gp.dma_start(
                                out=Aa[:, STG_0 + SOFF[j]:
                                       STG_0 + SOFF[j] + KBL[j] * 128],
                                in_=bass.AP(
                                    tensor=pp_dram.tensor,
                                    offset=128 * SUMW + int(WOFF[j]) + 127,
                                    ap=[[SUMW - 1, 128], [1, KBL[j] * 128]])))

                @blk.scalar
                def _(sc, p=p, pv0=pv0):
                    for h in range(2):
                        for u in range(4):
                            hu = h * 4 + u
                            for g in range(16):
                                G = hu * 16 + g
                                sc.wait_ge(smq.h, smq_d[(p, h, u, g)])
                                if G >= 3:
                                    sc.wait_ge(spv.h, pv0 + G - 2)
                                w = sc.activation(
                                    Bu[:, AR_0 + (G % 3) * 1024:
                                       AR_0 + (G % 3) * 1024 + 1024],
                                    ps[G % 2][:, 0:1024],
                                    AF.Exp, scale=0.125)
                                inc(sxp, w)

                @blk.vector
                def _(ve, p=p, pv0=pv0, dv0=dv0):
                    for h in range(2):
                        for u in range(4):
                            hu = h * 4 + u
                            ve.wait_ge(spv.h, pv0 + hu * 16 + 16)
                            w = ve.tensor_copy(
                                Oo[64:65, u * 512:u * 512 + 512],
                                ps[2][64:65, (u % 2) * 512:(u % 2) * 512 + 512])
                            inc(ssc, w)
                            ve.wait_ge(sbc.h, sbc_d[(p, h, u)])
                            if h == 1:
                                ve.wait_ge(sop.h, sop_d[(p, 0, NCH - 1, 3)])
                            ve.tensor_copy(Bu[0:64, 15424:15936],
                                           ps[3][0:64, (u % 2) * 512:
                                                 (u % 2) * 512 + 512])
                            ve.reciprocal(Bu[0:64, 15424:15936],
                                          Bu[0:64, 15424:15936])
                            w = ve.tensor_mul(
                                Bu[0:64, OT_0 + u * 512:OT_0 + u * 512 + 512],
                                Bu[0:64, 15424:15936],
                                ps[2][0:64, (u % 2) * 512:(u % 2) * 512 + 512])
                            inc(sdv, w)
                        for dc in range(NCH):
                            for u in range(4):
                                i = dc * 4 + u
                                ve.wait_ge(sop.h, sop_d[(p, h, dc, u)])
                                dst = XR[:, dc * TOK + u * 512:
                                         dc * TOK + u * 512 + 512]
                                if h == 0:
                                    w = ve.scalar_tensor_tensor(
                                        dst, ps[3][:, (i % 2) * 512:
                                                   (i % 2) * 512 + 512],
                                        obias[:, dc:dc + 1], dst,
                                        ALU.add, ALU.add)
                                else:
                                    w = ve.tensor_add(
                                        dst, dst,
                                        ps[3][:, (i % 2) * 512:
                                              (i % 2) * 512 + 512])
                                inc(sxa, w)

        return


    X8 = Bu[:, 10240:14336].bitcast(FP8)
    W1_8 = Aa[:, 0:8192].bitcast(FP8)
    W2_8 = Aa[:, 8192:16384].bitcast(FP8)
    H8 = Bu[:, 0:4096].bitcast(FP8)
    YST = Bu[:, 8192:8448].bitcast(BF16)
    MSK = Bu[:, 8448:9472].bitcast(BF16)
    HST = Bu[:, 9984:10240].bitcast(BF16)

    def emit_ffn0():
        sfw = [sem("fw0"), sem("fw1")]
        sfc = sem("fc")
        sfm = sem("fm")
        sfa = sem("fa")
        sf8 = sem("f8")
        sfk = sem("fk")
        sfy = sem("fy")
        fw0 = [sfw[0].v, sfw[1].v]
        fc0, fm0, fa0, f80, fk0 = sfc.v, sfm.v, sfa.v, sf8.v, sfk.v
        fy0 = sfy.v
        UN = KC + NCH

        with nc.Block(no_gpsimd_drain=True) as blk:
            @blk.sync
            def _(sy):
                for pc in range(8):
                    buf = pc % 2
                    if pc >= 2:
                        sy.wait_ge(sfc.h, fc0 + pc - 1)
                    dinc(sfw[buf], sy.dma_start(
                        out=Bu[:, buf * 4096:buf * 4096 + 4096],
                        in_=bass.AP(tensor=w1t.tensor, offset=pc * 512,
                                    ap=[[FF, 128], [128 * FF, 8], [1, 512]])))
                for pc in range(8):
                    rg, ch = pc // 2, pc % 2
                    buf = pc % 2
                    sy.wait_ge(sfc.h, fc0 + 8 + pc - 1)
                    dinc(sfw[buf], sy.dma_start(
                        out=Bu[:, buf * 4096:buf * 4096 + 4096],
                        in_=bass.AP(tensor=w2t.tensor,
                                    offset=rg * 8 * 128 * D + ch * 512,
                                    ap=[[D, 128], [128 * D, 8], [1, 512]])))
                for tt in range(4):
                    for dc in range(NCH):
                        i = tt * NCH + dc
                        if i >= 2:
                            tt2, dc2 = divmod(i - 2, NCH)
                            sy.wait_ge(sf8.h,
                                       f80 + tt2 * UN + KC + dc2 + 1)
                        dinc(sfk, sy.dma_start(
                            out=MSK[:, (i % 2) * 512:(i % 2) * 512 + 512],
                            in_=bass.AP(tensor=dropm.tensor,
                                        offset=dc * 128 * TOK + tt * 512,
                                        ap=[[TOK, 128], [1, 512]])))

            @blk.vector
            def _(ve):
                for pc in range(8):
                    buf = pc % 2
                    ve.wait_ge(sfw[buf].h, fw0[buf] + 16 * (pc // 2 + 1))
                    dst = W1_8.rearrange("p (a b) -> p a b", a=8)[
                        :, :, pc * 512:pc * 512 + 512]
                    src = Bu[:, buf * 4096:buf * 4096 + 4096].rearrange(
                        "p (a b) -> p a b", a=8)
                    inc(sfc, ve.tensor_copy(dst, src))
                for pc in range(8):
                    rg, ch = pc // 2, pc % 2
                    buf = pc % 2
                    ve.wait_ge(sfw[buf].h, fw0[buf] + 16 * (4 + pc // 2 + 1))
                    dst = W2_8.rearrange("p (a b) -> p a b", a=32)[
                        :, rg * 8:rg * 8 + 8, ch * 512:ch * 512 + 512]
                    src = Bu[:, buf * 4096:buf * 4096 + 4096].rearrange(
                        "p (a b) -> p a b", a=8)
                    inc(sfc, ve.tensor_copy(dst, src))
                for tt in range(4):
                    for hc in range(KC):
                        ve.wait_ge(sfa.h, fa0 + tt * KC + hc + 1)
                        if tt >= 1 and hc == 0:
                            ve.wait_ge(sfm.h, fm0 + tt * UN)
                        inc(sf8, ve.tensor_copy(
                            H8[:, hc * 512:hc * 512 + 512], HST[:, 0:512]))
                    for dc in range(NCH):
                        iy = tt * NCH + dc
                        ve.wait_ge(sfm.h, fm0 + tt * UN + KC + dc + 1)
                        ve.wait_ge(sfk.h, fk0 + 16 * (iy + 1))
                        w = ve.tensor_scalar(YST[:, 0:512],
                                             ps[2 + iy % 2][:, 0:512],
                                             B2C[:, dc:dc + 1], None, ALU.add)
                        inc(sfy, w)
                        ve.tensor_mul(Bu[:, 9472:9984], YST[:, 0:512],
                                      MSK[:, (iy % 2) * 512:
                                          (iy % 2) * 512 + 512])
                        w = ve.tensor_add(
                            XR[:, dc * TOK + tt * 512:
                               dc * TOK + tt * 512 + 512],
                            XR[:, dc * TOK + tt * 512:
                               dc * TOK + tt * 512 + 512],
                            Bu[:, 9472:9984])
                        inc(sf8, w)

            @blk.tensor
            def _(te):
                te.wait_ge(sfc.h, fc0 + 16)
                for tt in range(4):
                    for hc in range(KC):
                        i = tt * KC + hc
                        if i >= 2:
                            tt2, hc2 = divmod(i - 2, KC)
                            te.wait_ge(sfa.h, fa0 + tt2 * KC + hc2 + 1)
                        reg = ps[i % 2][:, 0:512]
                        for dc in range(NCH):
                            w = te.matmul(
                                reg,
                                W1_8[:, dc * 4096 + hc * 128:
                                     dc * 4096 + hc * 128 + 128],
                                X8[:, dc * 2048 + tt * 512:
                                   dc * 2048 + tt * 512 + 512],
                                start=(dc == 0), stop=(dc == NCH - 1))
                        inc(sfm, w)
                    te.wait_ge(sf8.h, f80 + tt * UN + KC)
                    for dc in range(NCH):
                        iy = tt * NCH + dc
                        if iy >= 2:
                            te.wait_ge(sfy.h, fy0 + iy - 1)
                        reg = ps[2 + iy % 2][:, 0:512]
                        for hc in range(KC):
                            w = te.matmul(
                                reg,
                                W2_8[:, hc * 1024 + dc * 128:
                                     hc * 1024 + dc * 128 + 128],
                                H8[:, hc * 512:hc * 512 + 512],
                                start=(hc == 0), stop=(hc == KC - 1))
                        inc(sfm, w)

            @blk.scalar
            def _(sc):
                for tt in range(4):
                    for hc in range(KC):
                        sc.wait_ge(sfm.h, fm0 + tt * UN + hc + 1)
                        i = tt * KC + hc
                        if i >= 1:
                            sc.wait_ge(sf8.h, f80 + tt * UN + hc
                                       if hc >= 1 else
                                       f80 + (tt - 1) * UN + KC)
                        w = sc.activation(HST[:, 0:512],
                                          ps[i % 2][:, 0:512],
                                          AF.Relu, bias=b1sb[:, hc:hc + 1])
                        inc(sfa, w)

    XH = Aa[:, 0:8192].bitcast(BF16)
    XL = Aa[:, 8192:16384].bitcast(BF16)
    SGN = Bu[:, 0:4096].bitcast(BF16)

    def emit_ffn1():
        sgw = [sem("gw0"), sem("gw1")]
        sgs = sem("gs")
        sgm = sem("gm")
        sga = sem("ga")
        gw0 = [sgw[0].v, sgw[1].v]
        gs0, gm0, ga0 = sgs.v, sgm.v, sga.v

        with nc.Block(no_gpsimd_drain=True) as blk:
            @blk.sync
            def _(sy):
                for pc in range(2):
                    dinc(sgw[pc], sy.dma_start(
                        out=Bu[:, 4096 + pc * 4096:4096 + pc * 4096 + 4096],
                        in_=bass.AP(tensor=wft.tensor, offset=pc * 512,
                                    ap=[[D, 128], [128 * D, 8], [1, 512]])))

            @blk.scalar
            def _(sc):
                for pc in range(2):
                    sc.wait_ge(sgw[pc].h, gw0[pc] + 16)
                    for dc8 in range(NCH):
                        w = sc.activation(
                            SGN[:, dc8 * 1024 + pc * 512:
                                dc8 * 1024 + pc * 512 + 512],
                            Bu[:, 4096 + pc * 4096 + dc8 * 512:
                               4096 + pc * 4096 + dc8 * 512 + 512],
                            AF.Sign)
                        inc(sgs, w)

            @blk.tensor
            def _(te):
                te.wait_ge(sgs.h, gs0 + 16)
                for dc in range(NCH):
                    for tt in range(4):
                        i = dc * 4 + tt
                        if i >= 2:
                            te.wait_ge(sga.h, ga0 + i - 1)
                        reg = ps[i % 2][:, 0:512]
                        for dc8 in range(NCH):
                            for hl, XXp in ((0, XH), (1, XL)):
                                w = te.matmul(
                                    reg,
                                    SGN[:, dc8 * 1024 + dc * 128:
                                        dc8 * 1024 + dc * 128 + 128],
                                    XXp[:, dc8 * 2048 + tt * 512:
                                        dc8 * 2048 + tt * 512 + 512],
                                    start=(dc8 == 0 and hl == 0),
                                    stop=(dc8 == NCH - 1 and hl == 1))
                        inc(sgm, w)

            @blk.vector
            def _(ve):
                for dc in range(NCH):
                    for tt in range(4):
                        i = dc * 4 + tt
                        ve.wait_ge(sgm.h, gm0 + i + 1)
                        w = ve.tensor_add(
                            XR[:, dc * TOK + tt * 512:
                               dc * TOK + tt * 512 + 512],
                            XR[:, dc * TOK + tt * 512:
                               dc * TOK + tt * 512 + 512],
                            ps[i % 2][:, 0:512])
                        inc(sga, w)

    def cb_xn(ve, c, half, t_ap, rb_ap):
        ve.tensor_mul(Aa[:, c * TOK + half * 1024:
                         c * TOK + half * 1024 + 1024], t_ap, rb_ap)

    def cb_x8(ve, c, half, t_ap, rb_ap):
        ve.tensor_tensor(X8[:, c * 2048 + half * 1024:
                            c * 2048 + half * 1024 + 1024],
                         t_ap, rb_ap, ALU.mult)

    def cb_zzn(ve, c, half, t_ap, rb_ap):
        sl = slice(c * TOK + half * 1024, c * TOK + half * 1024 + 1024)
        ve.tensor_mul(Bu[:, 6144:7168], t_ap, rb_ap)
        ve.tensor_add(XR[:, sl], XR[:, sl], Bu[:, 6144:7168])
        ve.tensor_scalar(Aa[:, sl], Bu[:, 6144:7168],
                         LNW[:, c:c + 1], LNB[:, c:c + 1],
                         ALU.mult, ALU.add)

    def cb_hilo(ve, c, half, t_ap, rb_ap):
        xh = XH[:, c * 2048 + half * 1024:c * 2048 + half * 1024 + 1024]
        xl = XL[:, c * 2048 + half * 1024:c * 2048 + half * 1024 + 1024]
        ve.tensor_mul(Bu[:, 6144:7168], t_ap, rb_ap)
        ve.tensor_copy(xh, Bu[:, 6144:7168])
        ve.tensor_copy(Bu[:, 7168:8192], xh)
        ve.tensor_tensor(xl, Bu[:, 6144:7168], Bu[:, 7168:8192],
                         ALU.subtract)

    def emit_tap(i):
        if DEBUG_TAPS < i:
            return
        with nc.Block(no_gpsimd_drain=True) as blk:
            @blk.sync
            def _(sy):
                st = sem("tap")
                for c in range(NCH):
                    dinc(st, sy.dma_start(out=taps[i][c],
                                          in_=XR[:, c * TOK:(c + 1) * TOK]))
                sy.wait_ge(st.h, st.v)

    phases = [
        lambda: emit_ln(cb_xn),
        lambda: emit_qkv(0),
        lambda: emit_allgather(),
        lambda: emit_attention(0),
        lambda: emit_tap(1),
        lambda: emit_ln(cb_x8),
        lambda: emit_ffn0(),
        lambda: emit_tap(2),
        lambda: emit_ln(cb_zzn),
        lambda: emit_qkv(1),
        lambda: emit_allgather(),
        lambda: emit_attention(1),
        lambda: emit_tap(3),
        lambda: emit_ln(cb_hilo),
        lambda: emit_ffn1(),
    ]
    for pi, ph in enumerate(phases):
        if pi >= PHASE_LIMIT:
            break
        ph()

    with nc.Block(no_gpsimd_drain=True) as blk:
        @blk.sync
        def _(sy):
            st = sem("stout")
            for c in range(NCH):
                dinc(st, sy.dma_start(out=out_t[c],
                                      in_=XR[:, c * TOK:(c + 1) * TOK]))
            sy.wait_ge(st.h, st.v)

    return nc, taps


def _prep_inputs(inputs, core):
    b, par = core // 2, core % 2
    chunks = [2 * j + par for j in range(16)]
    tok_idx = np.concatenate([np.arange(c * 128, c * 128 + 128)
                              for c in chunks])

    x = np.asarray(inputs["L0_attn_x"], np.float32)[b][tok_idx]
    xt = np.ascontiguousarray(x.T).reshape(NCH, 128, TOK)

    cos, sin = _rotary_tables()
    cos_c = np.ascontiguousarray(np.tile(cos[tok_idx].T, (2, 1)))
    sin_c = np.ascontiguousarray(np.tile(sin[tok_idx].T, (2, 1)))

    E = np.asarray(inputs["L0_attn_rel_pos_emb"], np.float32)
    ewm = np.zeros((128, SUMW), np.float32)
    for j in BAND_J:
        q0 = (2 * j + par) * 128
        w0 = 3968 - q0
        idx = np.minimum(np.maximum(np.arange(w0, w0 + WWIN[j]), 0), CLAMP)
        blkw = E[idx].T
        ewm[0:64, int(WOFF[j]):int(WOFF[j]) + WWIN[j]] = blkw
        ewm[64:128, int(WOFF[j]):int(WOFF[j]) + WWIN[j]] = blkw
    e2 = np.tile(E[CLAMP][:, None], (2, 1)).reshape(128, 1)

    keep = _dropout_keepmask()[b][tok_idx]
    mask = np.where(keep, np.float32(1.25), np.float32(0.0))
    dropm_ = np.ascontiguousarray(mask.T).reshape(NCH, 128, TOK).astype(
        ml_dtypes.bfloat16)

    biases = np.zeros((128, 64), np.float32)
    biases[:, 0:8] = _chunkvec(inputs["L0_attn_bo"])
    bqkv = np.asarray(inputs["L1_attn_b_qkv"], np.float32)
    biases[:, 8:16] = _chunkvec(bqkv[0:D])
    biases[:, 16:24] = _chunkvec(bqkv[D:2 * D])
    biases[:, 24] = 1e-5
    biases[:, 32:40] = _chunkvec(inputs["L1_attn_b_proj"])
    biases[:, 40:48] = _chunkvec(inputs["L1_attn_ln_w"])
    biases[:, 48:56] = _chunkvec(inputs["L1_attn_ln_b"])
    biases[:, 56:64] = _chunkvec(inputs["L0_ffn_b2"])

    prot = np.zeros((128, 128), np.float32)
    P1 = np.zeros((64, 64), np.float32)
    for i in range(32):
        P1[i, i + 32] = -1.0
        P1[i + 32, i] = 1.0
    prot[0:64, 0:64] = P1.T
    prot[64:128, 64:128] = P1.T

    return {
        "x_in": xt,
        "wq": np.asarray(inputs["L0_attn_wq"], np.float32),
        "wk": np.asarray(inputs["L0_attn_wk"], np.float32),
        "wv": np.asarray(inputs["L0_attn_wv"], np.float32),
        "wo": np.asarray(inputs["L0_attn_wo"], np.float32),
        "ew": ewm,
        "e2046": e2,
        "cos_t": cos_c,
        "sin_t": sin_c,
        "w1t": np.ascontiguousarray(np.asarray(inputs["L0_ffn_W1"],
                                               np.float32).T),
        "b1_t": _chunkvec(inputs["L0_ffn_b1"]),
        "w2t": np.ascontiguousarray(np.asarray(inputs["L0_ffn_W2"],
                                               np.float32).T),
        "dropm": dropm_,
        "wqkv": np.asarray(inputs["L1_attn_W_qkv"], np.float32),
        "bv_b": np.ascontiguousarray(np.broadcast_to(bqkv[2 * D:3 * D],
                                                     (128, D))),
        "wproj": np.asarray(inputs["L1_attn_W_proj"], np.float32),
        "wft": np.ascontiguousarray(np.asarray(inputs["L1_ffn_weight"],
                                               np.float32).T),
        "ident_in": np.eye(128, dtype=np.float32),
        "prot_in": prot,
        "ones_in": np.ones((128, 1), np.float32),
        "ones1x": np.ones((1, 128), np.float32),
        "ones64x": np.ones((1, 64), np.float32),
        "biases_in": biases,
    }


def kernel(**inputs):
    nc, _taps = build_nc()
    in_maps = [_prep_inputs(inputs, c) for c in range(8)]
    res = run_bass_kernel_spmd(nc, in_maps, core_ids=list(range(8)))
    outs = []
    for core in range(8):
        o = np.asarray(res.results[core]["out"])
        outs.append(o)
    full = np.zeros((B, T, D), np.float32)
    for core in range(8):
        b, par = core // 2, core % 2
        chunks = [2 * j + par for j in range(16)]
        tok_idx = np.concatenate([np.arange(c * 128, c * 128 + 128)
                                  for c in chunks])
        full[b][tok_idx] = outs[core].reshape(D, TOK).T
    return full


# version 08
# speedup vs baseline: 1.0138x; 1.0138x over previous
import numpy as np
import ml_dtypes
import concourse.bass as bass
import concourse.mybir as mybir
from concourse.bass_utils import run_bass_kernel_spmd

F32 = mybir.dt.float32
BF16 = mybir.dt.bfloat16
FP8 = mybir.dt.float8e5
AF = mybir.ActivationFunctionType
ALU = mybir.AluOpType

B, T, D = 4, 4096, 1024
H, DH = 16, 64
FF = 4096
TOK = 2048
NCH = 8
NPAIR = 8
KC = 32
CLAMP = 2046

KBL = [max(0, -(-(256 * j - 1794) // 128)) for j in range(16)]
WWIN = [kbl * 128 + 128 if kbl > 0 else 0 for kbl in KBL]
WOFF = np.concatenate([[0], np.cumsum(WWIN)]).astype(int)
SUMW = int(WOFF[-1])
BAND_J = [j for j in range(16) if KBL[j] > 0]
SOFF = {}
_acc = 0
for _j in BAND_J:
    SOFF[_j] = _acc
    _acc += KBL[_j] * 128
STAGE_W = _acc
PERM_GC = [2 * s for s in range(16)] + [2 * s + 1 for s in range(16)]

DEPTH_LIMIT = 2
DEBUG_TAPS = 0
PHASE_LIMIT = 99
ATTN_REL = True
ATTN_PAIRS = 8
ATTN_DELTA = True
ATTN_SKEW = True
ATTN_PP = True


def _threefry2x32(k1, k2, x0, x1):
    x0 = np.asarray(x0, np.uint32).copy()
    x1 = np.asarray(x1, np.uint32).copy()
    ks0 = np.uint32(k1)
    ks1 = np.uint32(k2)
    ks2 = np.uint32(0x1BD11BDA) ^ ks0 ^ ks1
    x0 += ks0
    x1 += ks1
    R = [[13, 15, 26, 6], [17, 29, 16, 24]]
    inj = [(ks1, ks2, 1), (ks2, ks0, 2), (ks0, ks1, 3), (ks1, ks2, 4),
           (ks2, ks0, 5)]
    for i in range(5):
        for r in R[i % 2]:
            x0 += x1
            x1 = (x1 << np.uint32(r)) | (x1 >> np.uint32(32 - r))
            x1 ^= x0
        a, b, c = inj[i]
        x0 += a
        x1 += b + np.uint32(c)
    return x0, x1


def _dropout_keepmask():
    n = B * T * D
    idx = np.arange(n, dtype=np.uint64)
    c1 = (idx >> np.uint64(32)).astype(np.uint32)
    c2 = (idx & np.uint64(0xFFFFFFFF)).astype(np.uint32)
    b1, b2 = _threefry2x32(0, 42, c1, c2)
    bits = b1 ^ b2
    u = ((bits >> np.uint32(9)) | np.uint32(0x3F800000)).view(np.float32) \
        - np.float32(1.0)
    return (u < np.float32(0.8)).reshape(B, T, D)


def _rotary_tables():
    inv = (1.0 / 10000.0 ** (np.arange(0, DH, 2, dtype=np.float32) / DH)
           ).astype(np.float32)
    t = np.arange(T, dtype=np.float32)
    freqs = np.einsum("i,j->ij", t, inv).astype(np.float32)
    emb = np.concatenate([freqs, freqs], -1)
    return np.cos(emb).astype(np.float32), np.sin(emb).astype(np.float32)


def _chunkvec(v):
    return np.ascontiguousarray(np.asarray(v, np.float32).reshape(-1, 128).T)


def build_nc():
    nc = bass.Bass("TRN2", target_bir_lowering=False, debug=False,
                   detect_race_conditions=False)

    def dinp(name, shape, dtype=F32):
        return nc.dram_tensor(name, shape, dtype, kind="ExternalInput").ap()

    x_in = dinp("x_in", [NCH, 128, TOK])
    wq = dinp("wq", [D, D])
    wk = dinp("wk", [D, D])
    wv = dinp("wv", [D, D])
    wo = dinp("wo", [D, D])
    ew = dinp("ew", [128, SUMW])
    e2046 = dinp("e2046", [128, 1])
    cos_t = dinp("cos_t", [128, TOK])
    sin_t = dinp("sin_t", [128, TOK])
    w1t = dinp("w1t", [D, FF])
    b1_t = dinp("b1_t", [128, FF // 128])
    w2t = dinp("w2t", [FF, D])
    dropm = dinp("dropm", [NCH, 128, TOK], BF16)
    wqkv = dinp("wqkv", [D, 3 * D])
    bv_b = dinp("bv_b", [128, D])
    wproj = dinp("wproj", [D, D])
    wft = dinp("wft", [D, D])
    ident_in = dinp("ident_in", [128, 128])
    prot_in = dinp("prot_in", [128, 128])
    ones_in = dinp("ones_in", [128, 1])
    ones1x = dinp("ones1x", [1, 128])
    ones64x = dinp("ones64x", [1, 64])
    biases_in = dinp("biases_in", [128, 64])

    out_t = nc.dram_tensor("out", [NCH, 128, TOK], F32,
                           kind="ExternalOutput").ap()
    taps = {}
    for i in range(1, DEBUG_TAPS + 1):
        taps[i] = nc.dram_tensor("tap%d" % i, [NCH, 128, TOK], F32,
                                 kind="ExternalOutput").ap()

    q_dram = nc.dram_tensor("q_dram", [NPAIR, 128, TOK], F32,
                            kind="Internal").ap()
    KSEC = NPAIR * 128 * TOK
    VSEC = (TOK // 128) * 128 * D
    cc_in = nc.dram_tensor("cc_in", [KSEC + VSEC], F32, kind="Internal").ap()
    cc_out = nc.dram_tensor("cc_out", [2, KSEC + VSEC], F32,
                            kind="Internal").ap()
    pp_dram = nc.dram_tensor("pp_dram", [2, 128, SUMW], F32,
                             kind="Internal").ap()

    XR = nc.alloc_sbuf_tensor("xres", [128, NCH * TOK], F32).ap()
    Aa = nc.alloc_sbuf_tensor("arA", [128, NCH * TOK], F32).ap()
    Bu = nc.alloc_sbuf_tensor("arB", [128, NCH * TOK], F32).ap()
    Oo = nc.alloc_sbuf_tensor("arO", [128, 2048], F32).ap()
    ident = nc.alloc_sbuf_tensor("ident", [128, 128], F32).ap()
    prot = nc.alloc_sbuf_tensor("prot", [128, 128], F32).ap()
    onesc = nc.alloc_sbuf_tensor("onesc", [128, 1], F32).ap()
    ones1 = nc.alloc_sbuf_tensor("ones1", [1, 128], F32).ap()
    ones64 = nc.alloc_sbuf_tensor("ones64", [128, 64], F32).ap()
    biases = nc.alloc_sbuf_tensor("biases", [128, 64], F32).ap()
    b1sb = nc.alloc_sbuf_tensor("b1sb", [128, FF // 128], F32).ap()
    bvb = nc.alloc_sbuf_tensor("bvb", [128, D], F32).ap()
    e2sb = nc.alloc_sbuf_tensor("e2sb", [128, 1], F32).ap()

    ps = [nc.alloc_psum_tensor("ps%d" % i, [128, 1024], F32).ap()
          for i in range(4)]

    BO = biases[:, 0:8]
    BQ = biases[:, 8:16]
    BK = biases[:, 16:24]
    BPROJ = biases[:, 32:40]
    LNW = biases[:, 40:48]
    LNB = biases[:, 48:56]
    B2C = biases[:, 56:64]
    EPS = biases[:, 24:25]

    sems = {}

    class S:
        __slots__ = ("h", "v")

        def __init__(self, name):
            self.h = nc.alloc_semaphore(name)
            self.v = 0

    def sem(name):
        if name not in sems:
            sems[name] = S(name)
        return sems[name]

    def inc(s, instr, n=1):
        instr.then_inc(s.h, n)
        s.v += n
        return s.v

    def dinc(s, instr):
        return inc(s, instr, 16)

    with nc.Block(no_gpsimd_drain=True) as blk:
        @blk.sync
        def _(sy):
            ld = sem("pre")
            for dst, src in [
                (ident, ident_in), (prot, prot_in), (onesc, ones_in),
                (ones1, ones1x), (biases, biases_in), (b1sb, b1_t),
                (bvb, bv_b), (e2sb, e2046),
            ]:
                dinc(ld, sy.dma_start(out=dst[:], in_=src[:]))
            dinc(ld, sy.dma_start(out=ones64[64:65, :], in_=ones64x[:]))
            for c in range(NCH):
                dinc(ld, sy.dma_start(out=XR[:, c * TOK:(c + 1) * TOK],
                                      in_=x_in[c]))
            sy.wait_ge(ld.h, ld.v)

    ln_n = [0]

    def emit_ln(apply_cb):
        i = ln_n[0]
        ln_n[0] += 1
        lq = sem("lnq")
        lm = sem("lnm")
        lr = sem("lnr")
        q00, m00, r00 = lq.v, lm.v, lr.v
        qb = {0: q00, 1: q00 + NCH + 1}
        mb = {0: m00, 1: m00 + NCH + 1}
        rb = {0: r00, 1: r00 + 3}
        with nc.Block(no_gpsimd_drain=True) as blk:
            @blk.scalar
            def _(sc):
                for half in (0, 1):
                    hb = half * 1024
                    q0, m0, r0 = qb[half], mb[half], rb[half]
                    for c in range(NCH):
                        if c >= 2 or half == 1:
                            sc.wait_ge(lm.h, m0 + max(0, c - 1))
                        inc(lq, sc.activation(
                            Bu[:, (c % 2) * 1024:(c % 2) * 1024 + 1024],
                            XR[:, c * TOK + hb:c * TOK + hb + 1024],
                            AF.Square))
                    sc.wait_ge(lr.h, r0 + 1)
                    inc(lq, sc.activation(
                        Bu[0:1, 4096 + half * 2048:5120 + half * 2048],
                        Bu[0:1, 4096 + half * 2048:5120 + half * 2048],
                        AF.Sqrt, bias=EPS[0:1, :]))

            @blk.tensor
            def _(te):
                for half in (0, 1):
                    hb = half * 1024
                    q0, m0, r0 = qb[half], mb[half], rb[half]
                    if half == 1:
                        te.wait_ge(lr.h, rb[0] + 1)
                    for c in range(NCH):
                        for n2 in range(2):
                            te.matmul(ps[2][0:1, n2 * 512:n2 * 512 + 512],
                                      onesc,
                                      XR[:, c * TOK + hb + n2 * 512:
                                         c * TOK + hb + n2 * 512 + 512],
                                      start=(c == 0), stop=(c == NCH - 1))
                        te.wait_ge(lq.h, q0 + c + 1)
                        for n2 in range(2):
                            w = te.matmul(
                                ps[3][0:1, n2 * 512:n2 * 512 + 512],
                                onesc,
                                Bu[:, (c % 2) * 1024 + n2 * 512:
                                   (c % 2) * 1024 + n2 * 512 + 512],
                                start=(c == 0), stop=(c == NCH - 1))
                        inc(lm, w)
                    te.wait_ge(lr.h, r0 + 2)
                    if half == 1:
                        te.wait_ge(lr.h, rb[0] + 3)
                    for n2 in range(2):
                        te.matmul(ps[0][:, n2 * 512:n2 * 512 + 512], ones1,
                                  Bu[0:1, 3072 + half * 2048 + n2 * 512:
                                     3072 + half * 2048 + n2 * 512 + 512],
                                  start=True, stop=True)
                        w = te.matmul(ps[1][:, n2 * 512:n2 * 512 + 512], ones1,
                                      Bu[0:1, 4096 + half * 2048 + n2 * 512:
                                         4096 + half * 2048 + n2 * 512 + 512],
                                      start=True, stop=True)
                    inc(lm, w)

            @blk.vector
            def _(ve):
                for half in (0, 1):
                    hb = half * 1024
                    q0, m0, r0 = qb[half], mb[half], rb[half]
                    ve.wait_ge(lm.h, m0 + NCH)
                    ve.tensor_scalar_mul(Bu[0:1, 3072 + half * 2048:
                                            4096 + half * 2048],
                                         ps[2][0:1, 0:1024], 1.0 / 1024.0)
                    ve.tensor_scalar_mul(Bu[0:1, 4096 + half * 2048:
                                            5120 + half * 2048],
                                         ps[3][0:1, 0:1024], 1.0 / 1024.0)
                    ve.tensor_mul(Bu[0:1, 7168:8192],
                                  Bu[0:1, 3072 + half * 2048:4096 + half * 2048],
                                  Bu[0:1, 3072 + half * 2048:4096 + half * 2048])
                    w = ve.tensor_sub(Bu[0:1, 4096 + half * 2048:
                                         5120 + half * 2048],
                                      Bu[0:1, 4096 + half * 2048:
                                         5120 + half * 2048],
                                      Bu[0:1, 7168:8192])
                    inc(lr, w)
                    ve.wait_ge(lq.h, q0 + NCH + 1)
                    w = ve.reciprocal(Bu[0:1, 4096 + half * 2048:
                                         5120 + half * 2048],
                                      Bu[0:1, 4096 + half * 2048:
                                         5120 + half * 2048])
                    inc(lr, w)
                    ve.wait_ge(lm.h, m0 + NCH + 1)
                    for c in range(NCH):
                        ve.tensor_sub(Bu[:, 2048:3072],
                                      XR[:, c * TOK + hb:c * TOK + hb + 1024],
                                      ps[0][:, 0:1024])
                        apply_cb(ve, c, half, Bu[:, 2048:3072],
                                 ps[1][:, 0:1024])
                    w = ve.engine_nop()
                    inc(lr, w)

    def emit_qkv(layer):
        sw = [sem("qw0"), sem("qw1")]
        scs = sem("qcs")
        smm = sem("qmm")
        smr = sem("qmr")
        sraw = sem("qraw")
        scb = sem("qcb")
        sst = sem("qst")

        if layer == 0:
            jobs = [(wq, 0, True, "q", 0, None), (wq, 512, True, "q", 4, None),
                    (wk, 0, True, "k", 0, None), (wk, 512, True, "k", 4, None)]
            vmat, vcol0, vbias = wv, 0, False
        else:
            jobs = [(wqkv, 0, False, "q", 0, BQ), (wqkv, 512, False, "q", 4, BQ),
                    (wqkv, D, False, "k", 0, BK),
                    (wqkv, D + 512, False, "k", 4, BK)]
            vmat, vcol0, vbias = wqkv, 2 * D, True

        def wsrc(mat_ap, col0):
            t = mat_ap.tensor
            ncol = t.shape[1]
            return bass.AP(tensor=t, offset=col0,
                           ap=[[ncol, 128], [128 * ncol, 8], [1, 512]])

        def dst_ap(dst, oc_glob, tt):
            tn = q_dram.tensor if dst == "q" else cc_in.tensor
            return bass.AP(tensor=tn, offset=oc_glob * 128 * TOK + tt * 512,
                           ap=[[TOK, 128], [1, 512]])

        mm0, mr0, cb0, st0 = smm.v, smr.v, scb.v, sst.v
        raw0 = sraw.v
        w0b = [sw[0].v, sw[1].v]
        rix = {}
        _r = 0
        for ji, job in enumerate(jobs):
            if job[2]:
                for s in range(16):
                    rix[(ji, s)] = _r
                    _r += 1
        RAW0, QF0, VS0 = 12288, 14336, 15360

        with nc.Block(no_gpsimd_drain=True) as blk:
            @blk.sync
            def _(sy):
                if layer == 0:
                    dinc(scs, sy.dma_start(out=Bu[:, 8192:8192 + TOK],
                                           in_=cos_t[:]))
                    dinc(scs, sy.dma_start(out=Bu[:, 10240:10240 + TOK],
                                           in_=sin_t[:]))
                for ji, job in enumerate(jobs):
                    buf = ji % 2
                    if ji >= 2:
                        sy.wait_ge(smm.h, mm0 + (ji - 1) * 16)
                    dinc(sw[buf], sy.dma_start(
                        out=Bu[:, buf * 4096:(buf + 1) * 4096],
                        in_=wsrc(job[0], job[1])))
                for vh in range(2):
                    jv = len(jobs) + vh
                    buf = jv % 2
                    sy.wait_ge(smm.h, mm0 + (jv - 1) * 16)
                    dinc(sw[buf], sy.dma_start(
                        out=Bu[:, buf * 4096:(buf + 1) * 4096],
                        in_=wsrc(vmat, vcol0 + vh * 512)))
            @blk.gpsimd
            def _(gp):
                for ji, job in enumerate(jobs):
                    for s in range(16):
                        g = ji * 16 + s
                        oc, tt = s // 4, s % 4
                        gp.wait_ge(scb.h, cb0 + g + 1)
                        dinc(sst, gp.dma_start(
                            out=dst_ap(job[3], job[4] + oc, tt),
                            in_=Bu[:, QF0 + (g % 2) * 512:
                                   QF0 + (g % 2) * 512 + 512]))
                for vh in range(2):
                    for tc in range(16):
                        gv = vh * 16 + tc
                        gp.wait_ge(scb.h, cb0 + 64 + gv + 1)
                        dinc(sst, gp.dma_start(
                            out=bass.AP(tensor=cc_in.tensor,
                                        offset=KSEC + tc * 128 * D + vh * 512,
                                        ap=[[D, 128], [1, 512]]),
                            in_=Bu[:, VS0 + (gv % 2) * 512:
                                   VS0 + (gv % 2) * 512 + 512]))

            @blk.tensor
            def _(te):
                if layer == 0:
                    te.wait_ge(scs.h, scs.v)

                def rot_mm(ji, gp):
                    sp = gp - ji * 16
                    te.wait_ge(sraw.h, raw0 + rix[(ji, sp)] + 1)
                    if gp >= 2:
                        te.wait_ge(scb.h, cb0 + gp - 2 + 1)
                    w = te.matmul(
                        ps[2][:, (gp % 2) * 512:(gp % 2) * 512 + 512],
                        prot,
                        Bu[:, RAW0 + (gp % 4) * 512:RAW0 + (gp % 4) * 512 + 512],
                        start=True, stop=True)
                    inc(smr, w)

                for ji, job in enumerate(jobs):
                    mat, col0, rot, dst, oc0, bias = job
                    buf = ji % 2
                    te.wait_ge(sw[buf].h, w0b[buf] + 16 * (ji // 2 + 1))
                    for s in range(16):
                        g = ji * 16 + s
                        oc, tt = s // 4, s % 4
                        r = g % 4
                        pp = ps[r // 2][:, (r % 2) * 512:(r % 2) * 512 + 512]
                        if g >= 4:
                            te.wait_ge(scb.h, cb0 + g - 4 + 1)
                        for dc in range(NCH):
                            w = te.matmul(
                                pp,
                                Bu[:, buf * 4096 + dc * 512 + oc * 128:
                                   buf * 4096 + dc * 512 + oc * 128 + 128],
                                Aa[:, dc * TOK + tt * 512:
                                   dc * TOK + tt * 512 + 512],
                                start=(dc == 0), stop=(dc == NCH - 1))
                        inc(smm, w)
                        if rot and s >= 1:
                            rot_mm(ji, g - 1)
                    if rot:
                        rot_mm(ji, ji * 16 + 15)
                te.wait_ge(scb.h, cb0 + 64)
                for vh in range(2):
                    jv = len(jobs) + vh
                    buf = jv % 2
                    te.wait_ge(sw[buf].h, w0b[buf] + 16 * (jv // 2 + 1))
                    for tc in range(16):
                        gv = vh * 16 + tc
                        r = gv % 4
                        pp = ps[2 + r // 2][:, (r % 2) * 512:(r % 2) * 512 + 512]
                        if gv >= 4:
                            te.wait_ge(scb.h, cb0 + 64 + gv - 4 + 1)
                        for dc in range(NCH):
                            w = te.matmul(
                                pp,
                                Aa[:, dc * TOK + tc * 128:
                                   dc * TOK + tc * 128 + 128],
                                Bu[:, buf * 4096 + dc * 512:
                                   buf * 4096 + dc * 512 + 512],
                                start=(dc == 0), stop=(dc == NCH - 1))
                        inc(smm, w)

            @blk.vector
            def _(ve):
                def combine(ji, gp):
                    sp = gp - ji * 16
                    ttp = sp % 4
                    ve.wait_ge(smr.h, mr0 + rix[(ji, sp)] + 1)
                    if gp >= 2:
                        ve.wait_ge(sst.h, st0 + 16 * (gp - 2 + 1))
                    ve.tensor_mul(
                        Bu[:, QF0 + (gp % 2) * 512:QF0 + (gp % 2) * 512 + 512],
                        Bu[:, RAW0 + (gp % 4) * 512:RAW0 + (gp % 4) * 512 + 512],
                        Bu[:, 8192 + ttp * 512:8192 + ttp * 512 + 512])
                    ve.tensor_mul(
                        Bu[:, VS0 + 512:VS0 + 1024],
                        ps[2][:, (gp % 2) * 512:(gp % 2) * 512 + 512],
                        Bu[:, 10240 + ttp * 512:10240 + ttp * 512 + 512])
                    w = ve.tensor_add(
                        Bu[:, QF0 + (gp % 2) * 512:QF0 + (gp % 2) * 512 + 512],
                        Bu[:, QF0 + (gp % 2) * 512:QF0 + (gp % 2) * 512 + 512],
                        Bu[:, VS0 + 512:VS0 + 1024])
                    inc(scb, w)

                for ji, job in enumerate(jobs):
                    mat, col0, rot, dst, oc0, bias = job
                    for s in range(16):
                        g = ji * 16 + s
                        oc, tt = s // 4, s % 4
                        r = g % 4
                        pp = ps[r // 2][:, (r % 2) * 512:(r % 2) * 512 + 512]
                        ve.wait_ge(smm.h, mm0 + g + 1)
                        if rot:
                            if g >= 4:
                                ve.wait_ge(smr.h, mr0 + rix[(ji, s)] - 4 + 1)
                            w = ve.tensor_copy(
                                Bu[:, RAW0 + (g % 4) * 512:
                                   RAW0 + (g % 4) * 512 + 512], pp)
                            inc(sraw, w)
                            if s >= 1:
                                combine(ji, g - 1)
                            if s == 15:
                                combine(ji, g)
                        else:
                            if g >= 2:
                                ve.wait_ge(sst.h, st0 + 16 * (g - 2 + 1))
                            w = ve.tensor_scalar(
                                Bu[:, QF0 + (g % 2) * 512:
                                   QF0 + (g % 2) * 512 + 512],
                                pp, bias[:, oc0 + oc:oc0 + oc + 1],
                                None, ALU.add)
                            inc(scb, w)
                for vh in range(2):
                    for tc in range(16):
                        gv = vh * 16 + tc
                        r = gv % 4
                        pp = ps[2 + r // 2][:, (r % 2) * 512:(r % 2) * 512 + 512]
                        ve.wait_ge(smm.h, mm0 + 64 + gv + 1)
                        if gv >= 2:
                            ve.wait_ge(sst.h, st0 + 16 * 64 + 16 * (gv - 2 + 1))
                        if vbias:
                            w = ve.tensor_add(
                                Bu[:, VS0 + (gv % 2) * 512:
                                   VS0 + (gv % 2) * 512 + 512],
                                pp, bvb[:, vh * 512:vh * 512 + 512])
                        else:
                            w = ve.tensor_copy(
                                Bu[:, VS0 + (gv % 2) * 512:
                                   VS0 + (gv % 2) * 512 + 512], pp)
                        inc(scb, w)

    def emit_allgather():
        scc = sem("cc")
        sst = sem("qst")
        with nc.Block(no_gpsimd_drain=True) as blk:
            @blk.gpsimd
            def _(gp):
                gp.wait_ge(sst.h, sst.v)
                w = gp.collective_compute(
                    "AllGather", ALU.bypass,
                    replica_groups=[[0, 1], [2, 3], [4, 5], [6, 7]],
                    ins=[cc_in[:]], outs=[cc_out[:]])
                inc(scc, w)
                gp.wait_ge(scc.h, scc.v)

    E8_0, PP_0, STG_0 = 0, 2304, 4608
    KP_0, VA_0, VB_0, QP_0, AR_0, OT_0 = 0, 4096, 6176, 8256, 10304, 13376

    def emit_attention(layer):
        wmat = wo if layer == 0 else wproj
        obias = BO if layer == 0 else BPROJ
        rel = (layer == 0) and ATTN_REL
        NB = len(BAND_J)
        NW = {j: -(-WWIN[j] // 512) for j in BAND_J}
        NPW = sum(NW.values())

        sld = sem("ald")
        se8 = sem("ae8")
        spp = sem("app")
        spe = sem("apev")
        spd = sem("apst")
        skw = sem("askw")
        smq = sem("amq")
        sxp = sem("axp")
        spv = sem("apv")
        ssc = sem("asc")
        sbc = sem("abc")
        sdv = sem("adv")
        sop = sem("aop")
        sxa = sem("axa")

        def kslice(p, half):
            return bass.AP(tensor=cc_out.tensor,
                           offset=half * (KSEC + VSEC) + p * 128 * TOK,
                           ap=[[TOK, 128], [1, TOK]])

        def vsrc(half, hglob):
            return bass.AP(tensor=cc_out.tensor,
                           offset=half * (KSEC + VSEC) + KSEC + hglob * 64,
                           ap=[[D, 128], [128 * D, 16], [1, 64]])

        def vdst(vbase, half):
            return bass.AP(tensor=Bu.tensor, offset=vbase + half * 16 * 65,
                           ap=[[NCH * TOK, 128], [65, 16], [1, 64]])

        def ppc(ji, h, wch):
            j = BAND_J[ji]
            return sum(2 * NW[BAND_J[k]] for k in range(ji)) \
                + h * NW[j] + wch + 1

        for p in range(min(NPAIR, ATTN_PAIRS)):
            ld0, e80 = sld.v, se8.v
            pp0, pe0, pd0, kw0 = spp.v, spe.v, spd.v, skw.v
            with nc.Block(no_gpsimd_drain=True) as blk:
                @blk.sync
                def _(sy, p=p, ld0=ld0, pp0=pp0, pe0=pe0):
                    sy.wait_ge(sem("qst").h, sems["qst"].v)
                    dinc(sld, sy.dma_start(out=Bu[:, QP_0:QP_0 + TOK],
                                           in_=q_dram[p]))
                    for h in range(2):
                        for half in range(2):
                            dinc(sld, sy.dma_start(
                                out=vdst(VA_0 if h == 0 else VB_0, half),
                                in_=vsrc(half, 2 * p + h)))
                    for h in range(2):
                        dinc(sld, sy.dma_start(
                            out=Oo[0:64, h * 1024:h * 1024 + 1024],
                            in_=bass.AP(tensor=wmat.tensor,
                                        offset=(p * 128 + h * 64) * D,
                                        ap=[[D, 64], [1, 1024]])))
                    if rel:
                        for ji, j in enumerate(BAND_J):
                            if ji >= 1:
                                sy.wait_ge(spp.h, pp0 + ppc(ji - 1, 1,
                                                            NW[BAND_J[ji - 1]]
                                                            - 1))
                            dinc(sld, sy.dma_start(
                                out=Aa[:, E8_0:E8_0 + WWIN[j]],
                                in_=ew[:, int(WOFF[j]):
                                       int(WOFF[j]) + WWIN[j]]))

                @blk.vector
                def _(ve, p=p, ld0=ld0, pp0=pp0, pd0=pd0):
                    ve.wait_ge(sld.h, ld0 + 16 * 5)
                    for vbase in (VA_0, VB_0):
                        ones_ap = bass.AP(tensor=Bu.tensor,
                                          offset=vbase + 64,
                                          ap=[[NCH * TOK, 128], [65, 32],
                                              [1, 1]])
                        ve.memset(ones_ap, 1.0)
                    if rel:
                        for ji, j in enumerate(BAND_J):
                            ve.wait_ge(sld.h, ld0 + 16 * (7 + ji + 1))
                            w = ve.tensor_scalar(
                                Aa[:, E8_0:E8_0 + WWIN[j]],
                                Aa[:, E8_0:E8_0 + WWIN[j]],
                                e2sb[:, 0:1], 8.0, ALU.subtract, ALU.mult)
                            inc(se8, w)
                            for h in range(2):
                                seqi = 2 * ji + h
                                if seqi >= 1:
                                    ve.wait_ge(spd.h, pd0 + 16 * seqi)
                                for wch in range(NW[j]):
                                    ve.wait_ge(spp.h, pp0 + ppc(ji, h, wch))
                                    wid = min(512, WWIN[j] - wch * 512)
                                    w = ve.tensor_copy(
                                        Aa[:, PP_0 + wch * 512:
                                           PP_0 + wch * 512 + wid],
                                        ps[3][:, (wch % 2) * 512:
                                              (wch % 2) * 512 + wid])
                                    inc(spe, w)


                if rel:
                    @blk.tensor
                    def _(te, p=p, e80=e80, pe0=pe0, ld0=ld0):
                        te.wait_ge(sld.h, ld0 + 16)
                        last_reg = {0: None, 1: None}
                        for ji, j in enumerate(BAND_J):
                            te.wait_ge(se8.h, e80 + ji + 1)
                            for h in range(2):
                                for wch in range(NW[j]):
                                    gidx = ppc(ji, h, wch)
                                    r = wch % 2
                                    if last_reg[r] is not None:
                                        te.wait_ge(spe.h, pe0 + last_reg[r])
                                    last_reg[r] = gidx
                                    wid = min(512, WWIN[j] - wch * 512)
                                    w = te.matmul(
                                        ps[3][:, r * 512:r * 512 + wid],
                                        Bu[h * 64:h * 64 + 64,
                                           QP_0 + j * 128:QP_0 + j * 128 + 128],
                                        Aa[h * 64:h * 64 + 64,
                                           E8_0 + wch * 512:
                                           E8_0 + wch * 512 + wid],
                                        start=True, stop=True)
                                    inc(spp, w)

                    @blk.gpsimd
                    def _(gp, p=p, pd0=pd0, pe0=pe0):
                        gp.wait_ge(skw.h, skw.v)
                        for ji, j in enumerate(BAND_J):
                            for h in range(2):
                                gp.wait_ge(spe.h, pe0 + ppc(ji, h, NW[j] - 1))
                                dinc(spd, gp.dma_start(
                                    out=bass.AP(
                                        tensor=pp_dram.tensor,
                                        offset=h * 128 * SUMW + int(WOFF[j]),
                                        ap=[[SUMW, 128], [1, WWIN[j]]]),
                                    in_=Aa[:, PP_0:PP_0 + WWIN[j]]))
                        for ji, j in enumerate(BAND_J):
                            gp.wait_ge(spd.h, pd0 + 16 * (2 * ji + 1))
                            if not ATTN_SKEW:
                                continue
                            dinc(skw, gp.dma_start(
                                out=Aa[:, STG_0 + SOFF[j]:
                                       STG_0 + SOFF[j] + KBL[j] * 128],
                                in_=bass.AP(
                                    tensor=pp_dram.tensor,
                                    offset=int(WOFF[j]) + 127,
                                    ap=[[SUMW - 1, 128], [1, KBL[j] * 128]])))

            mq0, xp0, pv0 = smq.v, sxp.v, spv.v
            sc0, bc0, dv0 = ssc.v, sbc.v, sdv.v
            op0, xa0 = sop.v, sxa.v
            kwA = kw0 + 16 * NB
            kwB = kw0 + 32 * NB
            smq_d = {}
            sbc_d = {}
            sop_d = {}
            ldm0 = sld.v
            with nc.Block(no_gpsimd_drain=True) as blk:
                @blk.sync
                def _(sy, p=p, mq0=mq0):
                    for h in range(2):
                        if h == 1:
                            sy.wait_ge(smq.h, mq0 + 64)
                        for half in range(2):
                            for dup in range(2):
                                dinc(sld, sy.dma_start(
                                    out=bass.AP(
                                        tensor=Bu.tensor,
                                        offset=KP_0 + half * TOK
                                        + dup * 64 * NCH * TOK,
                                        ap=[[NCH * TOK, 64], [1, TOK]]),
                                    in_=bass.AP(
                                        tensor=cc_out.tensor,
                                        offset=half * (KSEC + VSEC)
                                        + (p * 128 + h * 64) * TOK,
                                        ap=[[TOK, 64], [1, TOK]])))
                        for dup in range(2):
                            dinc(sld, sy.dma_start(
                                out=bass.AP(
                                    tensor=Bu.tensor,
                                    offset=QP_0 + dup * 64 * NCH * TOK,
                                    ap=[[NCH * TOK, 64], [1, TOK]]),
                                in_=bass.AP(
                                    tensor=q_dram.tensor,
                                    offset=(p * 128 + h * 64) * TOK,
                                    ap=[[TOK, 64], [1, TOK]])))

                @blk.tensor
                def _(te, p=p, ld0=ld0, ldm0=ldm0, mq0=mq0, xp0=xp0, pv0=pv0,
                      sc0=sc0, bc0=bc0, dv0=dv0, op0=op0, xa0=xa0,
                      kwA=kwA, kwB=kwB):
                    for h in range(2):
                        te.wait_ge(sld.h, ldm0 + 16 * 6 * (h + 1))
                        vbase = VA_0 if h == 0 else VB_0
                        if rel and ATTN_SKEW:
                            te.wait_ge(skw.h, kwA if h == 0 else kwB)
                        for u in range(4):
                            hu = h * 4 + u
                            for g in range(16):
                                G = hu * 16 + g
                                if G >= 2:
                                    te.wait_ge(sxp.h, xp0 + G - 1)
                                for j2 in range(2):
                                    kc = 2 * g + j2
                                    gc = PERM_GC[kc]
                                    dl = []
                                    if rel and ATTN_DELTA:
                                        for jj in range(4):
                                            j = 4 * u + jj
                                            if j in SOFF and gc < KBL[j]:
                                                dl.append((j, jj))
                                    w = te.matmul(
                                        ps[G % 2][:, j2 * 512:j2 * 512 + 512],
                                        Bu[j2 * 64:j2 * 64 + 64,
                                           KP_0 + kc * 128:KP_0 + kc * 128 + 128],
                                        Bu[j2 * 64:j2 * 64 + 64,
                                           QP_0 + u * 512:QP_0 + u * 512 + 512],
                                        start=True, stop=(len(dl) == 0))
                                    for di, (j, jj) in enumerate(dl):
                                        w = te.matmul(
                                            ps[G % 2][:, j2 * 512 + jj * 128:
                                                  j2 * 512 + jj * 128 + 128],
                                            Aa[:, STG_0 + SOFF[j] + gc * 128:
                                               STG_0 + SOFF[j] + gc * 128 + 128],
                                            ident,
                                            start=False,
                                            stop=(di == len(dl) - 1))
                                smq_d[(p, h, u, g)] = inc(smq, w)
                                te.wait_ge(sxp.h, xp0 + G + 1)
                                if g == 0:
                                    if hu >= 2:
                                        te.wait_ge(sdv.h, dv0 + hu - 1)
                                for j2 in range(2):
                                    kc = 2 * g + j2
                                    w = te.matmul(
                                        ps[2][0:65, (u % 2) * 512:
                                              (u % 2) * 512 + 512],
                                        Bu[:, vbase + kc * 65:
                                           vbase + kc * 65 + 65],
                                        Bu[:, AR_0 + (G % 3) * 1024 + j2 * 512:
                                           AR_0 + (G % 3) * 1024 + j2 * 512
                                           + 512],
                                        start=(g == 0 and j2 == 0),
                                        stop=(g == 15 and j2 == 1))
                                inc(spv, w)
                            te.wait_ge(ssc.h, sc0 + hu + 1)
                            if hu >= 2:
                                te.wait_ge(sdv.h, dv0 + hu - 1)
                            if h == 1 and u == 0:
                                te.wait_ge(sxa.h, xa0 + 32)
                            w = te.matmul(
                                ps[3][0:64, (u % 2) * 512:(u % 2) * 512 + 512],
                                ones64[64:65, 0:64],
                                Oo[64:65, u * 512:u * 512 + 512],
                                start=True, stop=True)
                            sbc_d[(p, h, u)] = inc(sbc, w)
                        te.wait_ge(sdv.h, dv0 + h * 4 + 4)
                        for dc in range(NCH):
                            for u in range(4):
                                i = dc * 4 + u
                                if i >= 2:
                                    te.wait_ge(sxa.h, xa0 + h * 32 + i - 1)
                                w = te.matmul(
                                    ps[3][:, (i % 2) * 512:(i % 2) * 512 + 512],
                                    Oo[0:64, h * 1024 + dc * 128:
                                       h * 1024 + dc * 128 + 128],
                                    Bu[0:64, OT_0 + u * 512:
                                       OT_0 + u * 512 + 512],
                                    start=True, stop=True)
                                sop_d[(p, h, dc, u)] = inc(sop, w)

                @blk.gpsimd
                def _(gp, p=p, pd0=pd0):
                    if rel and ATTN_SKEW:
                        gp.wait_ge(smq.h, smq_d[(p, 0, 3, 15)])
                        for ji, j in enumerate(BAND_J):
                            dinc(skw, gp.dma_start(
                                out=Aa[:, STG_0 + SOFF[j]:
                                       STG_0 + SOFF[j] + KBL[j] * 128],
                                in_=bass.AP(
                                    tensor=pp_dram.tensor,
                                    offset=128 * SUMW + int(WOFF[j]) + 127,
                                    ap=[[SUMW - 1, 128], [1, KBL[j] * 128]])))

                @blk.scalar
                def _(sc, p=p, pv0=pv0):
                    for h in range(2):
                        for u in range(4):
                            hu = h * 4 + u
                            for g in range(16):
                                G = hu * 16 + g
                                sc.wait_ge(smq.h, smq_d[(p, h, u, g)])
                                if G >= 3:
                                    sc.wait_ge(spv.h, pv0 + G - 2)
                                w = sc.activation(
                                    Bu[:, AR_0 + (G % 3) * 1024:
                                       AR_0 + (G % 3) * 1024 + 1024],
                                    ps[G % 2][:, 0:1024],
                                    AF.Exp, scale=0.125)
                                inc(sxp, w)

                @blk.vector
                def _(ve, p=p, pv0=pv0, dv0=dv0):
                    for h in range(2):
                        for u in range(4):
                            hu = h * 4 + u
                            ve.wait_ge(spv.h, pv0 + hu * 16 + 16)
                            w = ve.tensor_copy(
                                Oo[64:65, u * 512:u * 512 + 512],
                                ps[2][64:65, (u % 2) * 512:(u % 2) * 512 + 512])
                            inc(ssc, w)
                            ve.wait_ge(sbc.h, sbc_d[(p, h, u)])
                            if h == 1:
                                ve.wait_ge(sop.h, sop_d[(p, 0, NCH - 1, 3)])
                            ve.tensor_copy(Bu[0:64, 15424:15936],
                                           ps[3][0:64, (u % 2) * 512:
                                                 (u % 2) * 512 + 512])
                            ve.reciprocal(Bu[0:64, 15424:15936],
                                          Bu[0:64, 15424:15936])
                            w = ve.tensor_mul(
                                Bu[0:64, OT_0 + u * 512:OT_0 + u * 512 + 512],
                                Bu[0:64, 15424:15936],
                                ps[2][0:64, (u % 2) * 512:(u % 2) * 512 + 512])
                            inc(sdv, w)
                        for dc in range(NCH):
                            for u in range(4):
                                i = dc * 4 + u
                                ve.wait_ge(sop.h, sop_d[(p, h, dc, u)])
                                dst = XR[:, dc * TOK + u * 512:
                                         dc * TOK + u * 512 + 512]
                                if h == 0:
                                    w = ve.scalar_tensor_tensor(
                                        dst, ps[3][:, (i % 2) * 512:
                                                   (i % 2) * 512 + 512],
                                        obias[:, dc:dc + 1], dst,
                                        ALU.add, ALU.add)
                                else:
                                    w = ve.tensor_add(
                                        dst, dst,
                                        ps[3][:, (i % 2) * 512:
                                              (i % 2) * 512 + 512])
                                inc(sxa, w)

        return


    X8 = Bu[:, 10240:14336].bitcast(FP8)
    W1_8 = Aa[:, 0:8192].bitcast(FP8)
    W2_8 = Aa[:, 8192:16384].bitcast(FP8)
    H8 = Bu[:, 0:4096].bitcast(FP8)
    YST = Bu[:, 8192:8448].bitcast(BF16)
    MSK = Bu[:, 8448:9472].bitcast(BF16)
    HST = Bu[:, 9984:10240].bitcast(BF16)

    def emit_ffn0():
        sfw = [sem("fw0"), sem("fw1")]
        sfc = sem("fc")
        sfm = sem("fm")
        sfa = sem("fa")
        sf8 = sem("f8")
        sfk = sem("fk")
        sfy = sem("fy")
        fw0 = [sfw[0].v, sfw[1].v]
        fc0, fm0, fa0, f80, fk0 = sfc.v, sfm.v, sfa.v, sf8.v, sfk.v
        fy0 = sfy.v
        UN = KC + NCH

        with nc.Block(no_gpsimd_drain=True) as blk:
            @blk.sync
            def _(sy):
                for pc in range(8):
                    buf = pc % 2
                    if pc >= 2:
                        sy.wait_ge(sfc.h, fc0 + pc - 1)
                    dinc(sfw[buf], sy.dma_start(
                        out=Bu[:, buf * 4096:buf * 4096 + 4096],
                        in_=bass.AP(tensor=w1t.tensor, offset=pc * 512,
                                    ap=[[FF, 128], [128 * FF, 8], [1, 512]])))
                for pc in range(8):
                    rg, ch = pc // 2, pc % 2
                    buf = pc % 2
                    sy.wait_ge(sfc.h, fc0 + 8 + pc - 1)
                    dinc(sfw[buf], sy.dma_start(
                        out=Bu[:, buf * 4096:buf * 4096 + 4096],
                        in_=bass.AP(tensor=w2t.tensor,
                                    offset=rg * 8 * 128 * D + ch * 512,
                                    ap=[[D, 128], [128 * D, 8], [1, 512]])))
                for tt in range(4):
                    for dc in range(NCH):
                        i = tt * NCH + dc
                        if i >= 2:
                            tt2, dc2 = divmod(i - 2, NCH)
                            sy.wait_ge(sf8.h,
                                       f80 + tt2 * UN + KC + dc2 + 1)
                        dinc(sfk, sy.dma_start(
                            out=MSK[:, (i % 2) * 512:(i % 2) * 512 + 512],
                            in_=bass.AP(tensor=dropm.tensor,
                                        offset=dc * 128 * TOK + tt * 512,
                                        ap=[[TOK, 128], [1, 512]])))

            @blk.vector
            def _(ve):
                for pc in range(8):
                    buf = pc % 2
                    ve.wait_ge(sfw[buf].h, fw0[buf] + 16 * (pc // 2 + 1))
                    dst = W1_8.rearrange("p (a b) -> p a b", a=8)[
                        :, :, pc * 512:pc * 512 + 512]
                    src = Bu[:, buf * 4096:buf * 4096 + 4096].rearrange(
                        "p (a b) -> p a b", a=8)
                    inc(sfc, ve.tensor_copy(dst, src))
                for pc in range(8):
                    rg, ch = pc // 2, pc % 2
                    buf = pc % 2
                    ve.wait_ge(sfw[buf].h, fw0[buf] + 16 * (4 + pc // 2 + 1))
                    dst = W2_8.rearrange("p (a b) -> p a b", a=32)[
                        :, rg * 8:rg * 8 + 8, ch * 512:ch * 512 + 512]
                    src = Bu[:, buf * 4096:buf * 4096 + 4096].rearrange(
                        "p (a b) -> p a b", a=8)
                    inc(sfc, ve.tensor_copy(dst, src))
                for tt in range(4):
                    for hc in range(KC):
                        if tt == 0 and hc == 0:
                            ve.wait_ge(sfw[0].h, fw0[0] + 16 * 8)
                            ve.wait_ge(sfw[1].h, fw0[1] + 16 * 8)
                        ve.wait_ge(sfa.h, fa0 + tt * KC + hc + 1)
                        if tt >= 1 and hc == 0:
                            ve.wait_ge(sfm.h, fm0 + tt * UN)
                        inc(sf8, ve.tensor_copy(
                            H8[:, hc * 512:hc * 512 + 512], HST[:, 0:512]))
                    for dc in range(NCH):
                        iy = tt * NCH + dc
                        ve.wait_ge(sfm.h, fm0 + tt * UN + KC + dc + 1)
                        ve.wait_ge(sfk.h, fk0 + 16 * (iy + 1))
                        w = ve.tensor_scalar(YST[:, 0:512],
                                             ps[2 + iy % 2][:, 0:512],
                                             B2C[:, dc:dc + 1], None, ALU.add)
                        inc(sfy, w)
                        ve.tensor_mul(Bu[:, 9472:9984], YST[:, 0:512],
                                      MSK[:, (iy % 2) * 512:
                                          (iy % 2) * 512 + 512])
                        w = ve.tensor_add(
                            XR[:, dc * TOK + tt * 512:
                               dc * TOK + tt * 512 + 512],
                            XR[:, dc * TOK + tt * 512:
                               dc * TOK + tt * 512 + 512],
                            Bu[:, 9472:9984])
                        inc(sf8, w)

            @blk.tensor
            def _(te):
                te.wait_ge(sfc.h, fc0 + 8)
                for tt in range(4):
                    for hc in range(KC):
                        i = tt * KC + hc
                        if i >= 2:
                            tt2, hc2 = divmod(i - 2, KC)
                            te.wait_ge(sfa.h, fa0 + tt2 * KC + hc2 + 1)
                        reg = ps[i % 2][:, 0:512]
                        for dc in range(NCH):
                            w = te.matmul(
                                reg,
                                W1_8[:, dc * 4096 + hc * 128:
                                     dc * 4096 + hc * 128 + 128],
                                X8[:, dc * 2048 + tt * 512:
                                   dc * 2048 + tt * 512 + 512],
                                start=(dc == 0), stop=(dc == NCH - 1))
                        inc(sfm, w)
                    te.wait_ge(sf8.h, f80 + tt * UN + KC)
                    if tt == 0:
                        te.wait_ge(sfc.h, fc0 + 16)
                    for dc in range(NCH):
                        iy = tt * NCH + dc
                        if iy >= 2:
                            te.wait_ge(sfy.h, fy0 + iy - 1)
                        reg = ps[2 + iy % 2][:, 0:512]
                        for hc in range(KC):
                            w = te.matmul(
                                reg,
                                W2_8[:, hc * 1024 + dc * 128:
                                     hc * 1024 + dc * 128 + 128],
                                H8[:, hc * 512:hc * 512 + 512],
                                start=(hc == 0), stop=(hc == KC - 1))
                        inc(sfm, w)

            @blk.scalar
            def _(sc):
                for tt in range(4):
                    for hc in range(KC):
                        sc.wait_ge(sfm.h, fm0 + tt * UN + hc + 1)
                        i = tt * KC + hc
                        if i >= 1:
                            sc.wait_ge(sf8.h, f80 + tt * UN + hc
                                       if hc >= 1 else
                                       f80 + (tt - 1) * UN + KC)
                        w = sc.activation(HST[:, 0:512],
                                          ps[i % 2][:, 0:512],
                                          AF.Relu, bias=b1sb[:, hc:hc + 1])
                        inc(sfa, w)

    XH = Aa[:, 0:8192].bitcast(BF16)
    XL = Aa[:, 8192:16384].bitcast(BF16)
    SGN = Bu[:, 0:4096].bitcast(BF16)

    def emit_ffn1():
        sgw = [sem("gw0"), sem("gw1")]
        sgs = sem("gs")
        sgm = sem("gm")
        sga = sem("ga")
        gw0 = [sgw[0].v, sgw[1].v]
        gs0, gm0, ga0 = sgs.v, sgm.v, sga.v

        with nc.Block(no_gpsimd_drain=True) as blk:
            @blk.sync
            def _(sy):
                for pc in range(2):
                    dinc(sgw[pc], sy.dma_start(
                        out=Bu[:, 4096 + pc * 4096:4096 + pc * 4096 + 4096],
                        in_=bass.AP(tensor=wft.tensor, offset=pc * 512,
                                    ap=[[D, 128], [128 * D, 8], [1, 512]])))

            @blk.scalar
            def _(sc):
                for pc in range(2):
                    sc.wait_ge(sgw[pc].h, gw0[pc] + 16)
                    for dc8 in range(NCH):
                        w = sc.activation(
                            SGN[:, dc8 * 1024 + pc * 512:
                                dc8 * 1024 + pc * 512 + 512],
                            Bu[:, 4096 + pc * 4096 + dc8 * 512:
                               4096 + pc * 4096 + dc8 * 512 + 512],
                            AF.Sign)
                        inc(sgs, w)

            @blk.tensor
            def _(te):
                te.wait_ge(sgs.h, gs0 + 16)
                for dc in range(NCH):
                    for tt in range(4):
                        i = dc * 4 + tt
                        if i >= 2:
                            te.wait_ge(sga.h, ga0 + i - 1)
                        reg = ps[i % 2][:, 0:512]
                        for dc8 in range(NCH):
                            for hl, XXp in ((0, XH), (1, XL)):
                                w = te.matmul(
                                    reg,
                                    SGN[:, dc8 * 1024 + dc * 128:
                                        dc8 * 1024 + dc * 128 + 128],
                                    XXp[:, dc8 * 2048 + tt * 512:
                                        dc8 * 2048 + tt * 512 + 512],
                                    start=(dc8 == 0 and hl == 0),
                                    stop=(dc8 == NCH - 1 and hl == 1))
                        inc(sgm, w)

            @blk.vector
            def _(ve):
                for dc in range(NCH):
                    for tt in range(4):
                        i = dc * 4 + tt
                        ve.wait_ge(sgm.h, gm0 + i + 1)
                        w = ve.tensor_add(
                            XR[:, dc * TOK + tt * 512:
                               dc * TOK + tt * 512 + 512],
                            XR[:, dc * TOK + tt * 512:
                               dc * TOK + tt * 512 + 512],
                            ps[i % 2][:, 0:512])
                        inc(sga, w)

    def cb_xn(ve, c, half, t_ap, rb_ap):
        ve.tensor_mul(Aa[:, c * TOK + half * 1024:
                         c * TOK + half * 1024 + 1024], t_ap, rb_ap)

    def cb_x8(ve, c, half, t_ap, rb_ap):
        ve.tensor_tensor(X8[:, c * 2048 + half * 1024:
                            c * 2048 + half * 1024 + 1024],
                         t_ap, rb_ap, ALU.mult)

    def cb_zzn(ve, c, half, t_ap, rb_ap):
        sl = slice(c * TOK + half * 1024, c * TOK + half * 1024 + 1024)
        ve.tensor_mul(Bu[:, 6144:7168], t_ap, rb_ap)
        ve.tensor_add(XR[:, sl], XR[:, sl], Bu[:, 6144:7168])
        ve.tensor_scalar(Aa[:, sl], Bu[:, 6144:7168],
                         LNW[:, c:c + 1], LNB[:, c:c + 1],
                         ALU.mult, ALU.add)

    def cb_hilo(ve, c, half, t_ap, rb_ap):
        xh = XH[:, c * 2048 + half * 1024:c * 2048 + half * 1024 + 1024]
        xl = XL[:, c * 2048 + half * 1024:c * 2048 + half * 1024 + 1024]
        ve.tensor_mul(Bu[:, 6144:7168], t_ap, rb_ap)
        ve.tensor_copy(xh, Bu[:, 6144:7168])
        ve.tensor_copy(Bu[:, 7168:8192], xh)
        ve.tensor_tensor(xl, Bu[:, 6144:7168], Bu[:, 7168:8192],
                         ALU.subtract)

    def emit_tap(i):
        if DEBUG_TAPS < i:
            return
        with nc.Block(no_gpsimd_drain=True) as blk:
            @blk.sync
            def _(sy):
                st = sem("tap")
                for c in range(NCH):
                    dinc(st, sy.dma_start(out=taps[i][c],
                                          in_=XR[:, c * TOK:(c + 1) * TOK]))
                sy.wait_ge(st.h, st.v)

    phases = [
        lambda: emit_ln(cb_xn),
        lambda: emit_qkv(0),
        lambda: emit_allgather(),
        lambda: emit_attention(0),
        lambda: emit_tap(1),
        lambda: emit_ln(cb_x8),
        lambda: emit_ffn0(),
        lambda: emit_tap(2),
        lambda: emit_ln(cb_zzn),
        lambda: emit_qkv(1),
        lambda: emit_allgather(),
        lambda: emit_attention(1),
        lambda: emit_tap(3),
        lambda: emit_ln(cb_hilo),
        lambda: emit_ffn1(),
    ]
    for pi, ph in enumerate(phases):
        if pi >= PHASE_LIMIT:
            break
        ph()

    with nc.Block(no_gpsimd_drain=True) as blk:
        @blk.sync
        def _(sy):
            st = sem("stout")
            for c in range(NCH):
                dinc(st, sy.dma_start(out=out_t[c],
                                      in_=XR[:, c * TOK:(c + 1) * TOK]))
            sy.wait_ge(st.h, st.v)

    return nc, taps


def _prep_inputs(inputs, core):
    b, par = core // 2, core % 2
    chunks = [2 * j + par for j in range(16)]
    tok_idx = np.concatenate([np.arange(c * 128, c * 128 + 128)
                              for c in chunks])

    x = np.asarray(inputs["L0_attn_x"], np.float32)[b][tok_idx]
    xt = np.ascontiguousarray(x.T).reshape(NCH, 128, TOK)

    cos, sin = _rotary_tables()
    cos_c = np.ascontiguousarray(np.tile(cos[tok_idx].T, (2, 1)))
    sin_c = np.ascontiguousarray(np.tile(sin[tok_idx].T, (2, 1)))

    E = np.asarray(inputs["L0_attn_rel_pos_emb"], np.float32)
    ewm = np.zeros((128, SUMW), np.float32)
    for j in BAND_J:
        q0 = (2 * j + par) * 128
        w0 = 3968 - q0
        idx = np.minimum(np.maximum(np.arange(w0, w0 + WWIN[j]), 0), CLAMP)
        blkw = E[idx].T
        ewm[0:64, int(WOFF[j]):int(WOFF[j]) + WWIN[j]] = blkw
        ewm[64:128, int(WOFF[j]):int(WOFF[j]) + WWIN[j]] = blkw
    e2 = np.tile(E[CLAMP][:, None], (2, 1)).reshape(128, 1)

    keep = _dropout_keepmask()[b][tok_idx]
    mask = np.where(keep, np.float32(1.25), np.float32(0.0))
    dropm_ = np.ascontiguousarray(mask.T).reshape(NCH, 128, TOK).astype(
        ml_dtypes.bfloat16)

    biases = np.zeros((128, 64), np.float32)
    biases[:, 0:8] = _chunkvec(inputs["L0_attn_bo"])
    bqkv = np.asarray(inputs["L1_attn_b_qkv"], np.float32)
    biases[:, 8:16] = _chunkvec(bqkv[0:D])
    biases[:, 16:24] = _chunkvec(bqkv[D:2 * D])
    biases[:, 24] = 1e-5
    biases[:, 32:40] = _chunkvec(inputs["L1_attn_b_proj"])
    biases[:, 40:48] = _chunkvec(inputs["L1_attn_ln_w"])
    biases[:, 48:56] = _chunkvec(inputs["L1_attn_ln_b"])
    biases[:, 56:64] = _chunkvec(inputs["L0_ffn_b2"])

    prot = np.zeros((128, 128), np.float32)
    P1 = np.zeros((64, 64), np.float32)
    for i in range(32):
        P1[i, i + 32] = -1.0
        P1[i + 32, i] = 1.0
    prot[0:64, 0:64] = P1.T
    prot[64:128, 64:128] = P1.T

    return {
        "x_in": xt,
        "wq": np.asarray(inputs["L0_attn_wq"], np.float32),
        "wk": np.asarray(inputs["L0_attn_wk"], np.float32),
        "wv": np.asarray(inputs["L0_attn_wv"], np.float32),
        "wo": np.asarray(inputs["L0_attn_wo"], np.float32),
        "ew": ewm,
        "e2046": e2,
        "cos_t": cos_c,
        "sin_t": sin_c,
        "w1t": np.ascontiguousarray(np.asarray(inputs["L0_ffn_W1"],
                                               np.float32).T),
        "b1_t": _chunkvec(inputs["L0_ffn_b1"]),
        "w2t": np.ascontiguousarray(np.asarray(inputs["L0_ffn_W2"],
                                               np.float32).T),
        "dropm": dropm_,
        "wqkv": np.asarray(inputs["L1_attn_W_qkv"], np.float32),
        "bv_b": np.ascontiguousarray(np.broadcast_to(bqkv[2 * D:3 * D],
                                                     (128, D))),
        "wproj": np.asarray(inputs["L1_attn_W_proj"], np.float32),
        "wft": np.ascontiguousarray(np.asarray(inputs["L1_ffn_weight"],
                                               np.float32).T),
        "ident_in": np.eye(128, dtype=np.float32),
        "prot_in": prot,
        "ones_in": np.ones((128, 1), np.float32),
        "ones1x": np.ones((1, 128), np.float32),
        "ones64x": np.ones((1, 64), np.float32),
        "biases_in": biases,
    }


def kernel(**inputs):
    nc, _taps = build_nc()
    in_maps = [_prep_inputs(inputs, c) for c in range(8)]
    res = run_bass_kernel_spmd(nc, in_maps, core_ids=list(range(8)))
    outs = []
    for core in range(8):
        o = np.asarray(res.results[core]["out"])
        outs.append(o)
    full = np.zeros((B, T, D), np.float32)
    for core in range(8):
        b, par = core // 2, core % 2
        chunks = [2 * j + par for j in range(16)]
        tok_idx = np.concatenate([np.arange(c * 128, c * 128 + 128)
                                  for c in chunks])
        full[b][tok_idx] = outs[core].reshape(D, TOK).T
    return full
